# version 01
# baseline (speedup 1.0000x reference)
import numpy as np
import ml_dtypes
import concourse.bass as bass
import concourse.mybir as mybir
from concourse.bass_utils import run_bass_kernel_spmd
from contextlib import ExitStack

F32 = mybir.dt.float32
BF16 = mybir.dt.bfloat16
F16 = mybir.dt.float16
I32 = mybir.dt.int32
AF = mybir.ActivationFunctionType
ALU = mybir.AluOpType
AX = mybir.AxisListType

NCORES = 8
S = 8192
D = 2048
TOWN = 1024
RUN = 512



def own_token_index(c):
    r0, r1 = c, 15 - c
    return np.concatenate([np.arange(RUN * r0, RUN * (r0 + 1)),
                           np.arange(RUN * r1, RUN * (r1 + 1))])


def _kglob_full():
    idx = np.empty(S, dtype=np.int64)
    for r in range(NCORES):
        idx[r * TOWN:(r + 1) * TOWN] = own_token_index(r)
    return idx


KGLOB = _kglob_full()


def run_owner_offset(g):
    r = g if g <= 7 else 15 - g
    ri = 0 if g <= 7 else 1
    return r * TOWN + ri * RUN


def perm256(n):
    j = np.arange(n)
    blk, off = j // 256, j % 256
    src = np.where(off < 128, 2 * (off % 128), 2 * (off - 128) + 1)
    return blk * 256 + src


PERM_D = perm256(D)


def bfc(x):
    return np.ascontiguousarray(np.asarray(x, dtype=np.float32).astype(ml_dtypes.bfloat16))


def f32c(x):
    return np.ascontiguousarray(x, dtype=np.float32)


def make_consts():
    C = {}
    C["I128"] = bfc(np.eye(128))
    C["ones_col"] = bfc(np.ones((128, 1)))
    C["ones_row"] = bfc(np.ones((1, 128)))
    e8 = np.zeros((8, 128), np.float32)
    e8[np.arange(128) // 16, np.arange(128)] = 1.0
    C["E8"] = f32c(e8)
    inv = 1.0 / (10000.0 ** (np.arange(0, 128, 2, dtype=np.float64) / 128))
    t = np.arange(S, dtype=np.float64)
    ang = np.outer(t, inv)
    C["cos0"] = f32c(np.cos(ang).T)
    C["sin0"] = f32c(np.sin(ang).T)
    inv1 = 1.0 / (10000.0 ** (np.arange(0, 64, 2, dtype=np.float64) / 64))
    f1 = np.outer(t, inv1)
    emb = np.concatenate([f1, f1], axis=1)
    C["cos1"] = f32c(np.cos(emb).T)
    C["sin1"] = f32c(np.sin(emb).T)
    Pd = np.zeros((64, 64), np.float32)
    for j in range(64):
        Pd[(2 * j) if j < 32 else (2 * (j - 32) + 1), j] = 1.0
    C["Pdeint"] = bfc(Pd)
    Pr = np.zeros((64, 64), np.float32)
    for j in range(64):
        if j < 32:
            Pr[2 * j + 1, j] = -1.0
        else:
            Pr[2 * (j - 32), j] = 1.0
    C["Prot"] = bfc(Pr)
    C["kglob"] = f32c(KGLOB.reshape(64, 128).T)
    for off in (0, 128, 256, 384):
        pp = np.arange(128)[:, None]
        ff = np.arange(512)[None, :]
        C[f"mask{off}"] = bfc(np.where(off + pp > ff, -30000.0, 0.0))
    PEe = np.zeros((128, 128), np.float32)
    POe = np.zeros((128, 128), np.float32)
    PEo = np.zeros((128, 128), np.float32)
    POo = np.zeros((128, 128), np.float32)
    for j in range(64):
        PEe[2 * j, j] = 1.0
        POe[2 * j, 64 + j] = 1.0
        PEo[2 * j + 1, j] = 1.0
        POo[2 * j + 1, 64 + j] = 1.0
    C["PEe"], C["POe"], C["PEo"], C["POo"] = bfc(PEe), bfc(POe), bfc(PEo), bfc(POo)
    return C


CONSTS = make_consts()


def mm_acc(t, ps, pairs, ncols, row_sl=None):
    mm = None
    for c0 in range(0, ncols, 512):
        c1 = min(c0 + 512, ncols)
        for j, (lh, rh) in enumerate(pairs):
            mm = t.matmul(ps[:, c0:c1], lh, rh[:, c0:c1],
                          start=(j == 0), stop=(j == len(pairs) - 1))
    return mm


def B(nc, **sections):
    with nc.Block() as blk:
        for name, fn in sections.items():
            getattr(blk, name)(fn)


class P:
    def __init__(self):
        self.nc = bass.Bass("TRN2", target_bir_lowering=False)
        self.ctx = ExitStack()
        self.dcnt = 0

    def inp(self, name, shape, dtype):
        return self.nc.dram_tensor(name, list(shape), dtype, kind="ExternalInput").ap()

    def out(self, name, shape, dtype):
        return self.nc.dram_tensor(name, list(shape), dtype, kind="ExternalOutput").ap()

    def dram(self, name, shape, dtype):
        return self.nc.dram_tensor(name, list(shape), dtype, kind="Internal").ap()

    def sb(self, name, shape, dtype, ctx=None):
        return (ctx or self.ctx).enter_context(
            self.nc.sbuf_tensor(name, list(shape), dtype))

    def ps(self, name, shape, dtype=F32, ctx=None):
        return (ctx or self.ctx).enter_context(
            self.nc.psum_tensor(name, list(shape), dtype))

    def sem(self, name):
        return self.ctx.enter_context(self.nc.semaphore(name))



def build_pre0():
    p = P()
    nc = p.nc
    xT = p.inp("xT", [D, TOWN], F32)
    Wq = p.inp("Wq", [D, 2048], F32)
    Wk = p.inp("Wk", [D, 512], F32)
    Wv = p.inp("Wv", [D, 512], F32)
    WA = p.inp("WA", [D, 32], F32)
    WB = p.inp("WB", [D, 32], F32)
    cos0 = p.inp("cos0", [64, TOWN], F32)
    sin0 = p.inp("sin0", [64, TOWN], F32)
    c_ones_col = p.inp("ones_col", [128, 1], BF16)
    c_ones_row = p.inp("ones_row", [1, 128], BF16)

    o_xn = p.out("xnT", [D, TOWN], BF16)
    o_q = p.out("qT", [2048, TOWN], BF16)
    o_k = p.out("kT", [512, TOWN], BF16)
    o_v = p.out("v", [TOWN, 512], BF16)
    o_a = p.out("aT", [32, TOWN], BF16)
    o_b = p.out("bT", [32, TOWN], BF16)
    o_qn2 = p.out("qn2", [1, 16 * TOWN], F32)
    o_kn2 = p.out("kn2", [1, 4], F32)

    xn_all = p.sb("xn_all", [128, 16 * TOWN], BF16)
    xn = [xn_all[:, i * TOWN:(i + 1) * TOWN] for i in range(16)]
    onec = p.sb("onec", [128, 1], BF16)
    onesr = p.sb("onesr", [1, 128], BF16)
    cosb = p.sb("cosb", [64, TOWN], F32)
    sinb = p.sb("sinb", [64, TOWN], F32)
    row = p.sb("row", [1, 4 * TOWN], F32)
    rowb = p.sb("rowb", [1, 2 * TOWN], BF16)
    f32tmp = p.sb("f32tmp", [128, TOWN], F32)
    tA = p.sb("tA", [64, TOWN], F32)
    tB = p.sb("tB", [64, TOWN], F32)
    obf_all = p.sb("obf_all", [128, 2 * TOWN], BF16)
    obf = [obf_all[:, 0:TOWN], obf_all[:, TOWN:2 * TOWN]]
    dsem = p.sem("dsem")
    tsem = p.sem("tsem")
    vsem = p.sem("vsem")
    stsem = p.sem("stsem")

    def dma(eng, out, in_):
        eng.dma_start(out=out, in_=in_).then_inc(dsem, 16)
        p.dcnt += 16

    with nc.Block() as blk:
        @blk.sync
        def _(sy):
            dma(sy, onec[:, :], c_ones_col)
            dma(sy, onesr[:, :], c_ones_row)
            dma(sy, cosb[:, :], cos0)
            dma(sy, sinb[:, :], sin0)

    ln_ctx = ExitStack()
    xb_all = p.sb("xb_all", [128, 16 * TOWN], BF16, ln_ctx)
    sq_all = p.sb("sq_all", [128, 16 * TOWN], BF16, ln_ctx)
    ps_s = p.ps("ps_s", [1, TOWN], ctx=ln_ctx)
    ps_q = p.ps("ps_q", [1, TOWN], ctx=ln_ctx)
    ps_a = p.ps("ps_a", [128, TOWN], ctx=ln_ctx)
    ps_b = p.ps("ps_b", [128, TOWN], ctx=ln_ctx)
    xb = [xb_all[:, i * TOWN:(i + 1) * TOWN] for i in range(16)]
    sq = [sq_all[:, i * TOWN:(i + 1) * TOWN] for i in range(16)]

    with nc.Block() as blk:
        @blk.gpsimd
        def _(g):
            for i in range(16):
                dma(g, xb[i], xT[128 * i:128 * (i + 1), :])
            g.wait_ge(dsem, p.dcnt)
    with nc.Block() as blk:
        @blk.scalar
        def _(s):
            for i in range(16):
                s.activation(sq[i], xb[i], AF.Square)
    with nc.Block() as blk:
        @blk.tensor
        def _(t):
            mm_acc(t, ps_s[:, :], [(onec[:, :], xb[i]) for i in range(16)], TOWN)
            mm_acc(t, ps_q[:, :], [(onec[:, :], sq[i]) for i in range(16)], TOWN)
    with nc.Block() as blk:
        @blk.vector
        def _(v):
            mu = row[0:1, 0:TOWN]
            e2 = row[0:1, TOWN:2 * TOWN]
            var = row[0:1, 2 * TOWN:3 * TOWN]
            v.tensor_scalar(mu, ps_s[0:1, :], 1.0 / D, None, ALU.mult)
            v.tensor_scalar(e2, ps_q[0:1, :], 1.0 / D, None, ALU.mult)
            v.tensor_tensor(var, mu, mu, ALU.mult)
            v.tensor_tensor(var, e2, var, ALU.subtract)
            v.tensor_single_scalar(var, var, 1e-5, ALU.add)
    with nc.Block() as blk:
        @blk.scalar
        def _(s):
            s.activation(row[0:1, 3 * TOWN:4 * TOWN],
                         row[0:1, 2 * TOWN:3 * TOWN], AF.Ln)
    with nc.Block() as blk:
        @blk.scalar
        def _(s):
            s.activation(row[0:1, 3 * TOWN:4 * TOWN],
                         row[0:1, 3 * TOWN:4 * TOWN], AF.Exp, scale=-0.5)
    with nc.Block() as blk:
        @blk.vector
        def _(v):
            mu = row[0:1, 0:TOWN]
            a = row[0:1, 3 * TOWN:4 * TOWN]
            v.tensor_copy(rowb[0:1, 0:TOWN], a)
            v.tensor_tensor(row[0:1, TOWN:2 * TOWN], mu, a, ALU.mult)
            v.tensor_scalar(rowb[0:1, TOWN:2 * TOWN],
                            row[0:1, TOWN:2 * TOWN], -1.0, None, ALU.mult)
    with nc.Block() as blk:
        @blk.tensor
        def _(t):
            mm_acc(t, ps_a[:, :], [(onesr[:, :], rowb[0:1, 0:TOWN])], TOWN)
            mm_acc(t, ps_b[:, :], [(onesr[:, :], rowb[0:1, TOWN:2 * TOWN])], TOWN)
    with nc.Block() as blk:
        @blk.vector
        def _(v):
            for i in range(16):
                v.tensor_tensor(f32tmp[:, :], xb[i], ps_a[:, :], ALU.mult)
                v.tensor_tensor(xn[i], f32tmp[:, :], ps_b[:, :], ALU.add)
    with nc.Block() as blk:
        @blk.sync
        def _(sy):
            for i in range(16):
                dma(sy, o_xn[128 * i:128 * (i + 1), :], xn[i])
    ln_ctx.close()

    pj_ctx = ExitStack()
    wq_all = p.sb("wq_all", [128, 16 * 2048], BF16, pj_ctx)
    wk_all = p.sb("wk_all", [128, 16 * 512], BF16, pj_ctx)
    wv_all = p.sb("wv_all", [128, 16 * 512], BF16, pj_ctx)
    wab_all = p.sb("wab_all", [128, 16 * 64], BF16, pj_ctx)
    ps_mm = [p.ps(f"ps_mm{i}", [128, TOWN], ctx=pj_ctx) for i in range(2)]
    wq = [wq_all[:, i * 2048:(i + 1) * 2048] for i in range(16)]
    wk = [wk_all[:, i * 512:(i + 1) * 512] for i in range(16)]
    wv = [wv_all[:, i * 512:(i + 1) * 512] for i in range(16)]
    wa = [wab_all[:, i * 64:i * 64 + 32] for i in range(16)]
    wb = [wab_all[:, i * 64 + 32:(i + 1) * 64] for i in range(16)]

    with nc.Block() as blk:
        @blk.gpsimd
        def _(g):
            for i in range(16):
                dma(g, wq[i], Wq[128 * i:128 * (i + 1), :])
                dma(g, wk[i], Wk[128 * i:128 * (i + 1), :])
                dma(g, wv[i], Wv[128 * i:128 * (i + 1), :])
                dma(g, wa[i], WA[128 * i:128 * (i + 1), :])
                dma(g, wb[i], WB[128 * i:128 * (i + 1), :])
            g.wait_ge(dsem, p.dcnt)

    jobs = ([("rope", o_q, o, wq, 128) for o in range(16)] +
            [("rope", o_k, o, wk, 128) for o in range(4)] +
            [("vmm", o_v, tv, wv, None) for tv in range(8)] +
            [("ab", o_a, 0, wa, 32), ("ab", o_b, 0, wb, 32)])

    with nc.Block() as blk:
        @blk.tensor
        def _(t):
            for j, (kind, dst, o, wt, _wsl) in enumerate(jobs):
                if j >= 2:
                    t.wait_ge(vsem, j - 1)
                pb = ps_mm[j % 2]
                if kind == "rope":
                    mm = mm_acc(t, pb[:, :],
                                [(wt[i][:, 128 * o:128 * (o + 1)], xn[i])
                                 for i in range(16)], TOWN)
                    mm.then_inc(tsem, 1)
                elif kind == "vmm":
                    mm = mm_acc(t, pb[0:128, 0:512],
                                [(xn[i][:, 128 * o:128 * (o + 1)], wt[i])
                                 for i in range(16)], 512)
                    mm.then_inc(tsem, 1)
                else:
                    mm = mm_acc(t, pb[0:32, :], [(wt[i], xn[i]) for i in range(16)],
                                TOWN)
                    mm.then_inc(tsem, 1)

        @blk.vector
        def _(v):
            for j, (kind, dst, o, wt, _wsl) in enumerate(jobs):
                v.wait_ge(tsem, j + 1)
                if j >= 2:
                    v.wait_ge(stsem, 16 * (j - 1))
                pb = ps_mm[j % 2]
                ob = obf[j % 2]
                if kind == "rope":
                    lo, hi = pb[0:64, :], pb[64:128, :]
                    v.tensor_tensor(tA[:, :], lo, cosb[:, :], ALU.mult)
                    v.tensor_tensor(tB[:, :], hi, sinb[:, :], ALU.mult)
                    v.tensor_tensor(ob[0:64, :], tA[:, :], tB[:, :], ALU.subtract)
                    v.tensor_tensor(tA[:, :], hi, cosb[:, :], ALU.mult)
                    v.tensor_tensor(tB[:, :], lo, sinb[:, :], ALU.mult)
                    v.tensor_tensor(ob[64:128, :], tA[:, :], tB[:, :],
                                    ALU.add).then_inc(vsem, 1)
                elif kind == "vmm":
                    v.tensor_copy(ob[:, 0:512], pb[0:128, 0:512]).then_inc(vsem, 1)
                else:
                    v.tensor_copy(ob[0:32, :], pb[0:32, :]).then_inc(vsem, 1)

        @blk.sync
        def _(sy):
            for j, (kind, dst, o, wt, _wsl) in enumerate(jobs):
                sy.wait_ge(vsem, j + 1)
                ob = obf[j % 2]
                if kind == "rope":
                    sy.dma_start(out=dst[128 * o:128 * (o + 1), :],
                                 in_=ob[:, :]).then_inc(stsem, 16)
                elif kind == "vmm":
                    sy.dma_start(out=dst[128 * o:128 * (o + 1), :],
                                 in_=ob[:, 0:512]).then_inc(stsem, 16)
                else:
                    sy.dma_start(out=dst, in_=ob[0:32, :]).then_inc(stsem, 16)
            sy.wait_ge(stsem, 16 * len(jobs))
    pj_ctx.close()

    n_ctx = ExitStack()
    ps_row = p.ps("ps_row", [1, TOWN], ctx=n_ctx)
    qh_all = p.sb("qh_all", [128, 20 * TOWN], BF16, n_ctx)
    sqn = p.sb("sqn", [128, TOWN], F32, n_ctx)
    onec_f = p.sb("onec_f", [128, 1], F32, n_ctx)
    qn2_sb = p.sb("qn2_sb", [1, 16 * TOWN], F32, n_ctx)
    kn2_sb = p.sb("kn2_sb", [1, 4], F32, n_ctx)
    qh = [qh_all[:, i * TOWN:(i + 1) * TOWN] for i in range(20)]
    with nc.Block() as blk:
        @blk.sync
        def _(sy):
            for h in range(16):
                dma(sy, qh[h], o_q[128 * h:128 * (h + 1), :])
            for h in range(4):
                dma(sy, qh[16 + h], o_k[128 * h:128 * (h + 1), :])
            sy.wait_ge(dsem, p.dcnt)

        @blk.vector
        def _(v):
            v.memset(onec_f[:, :], 1.0)
    for h in range(20):
        with nc.Block() as blk:
            @blk.scalar
            def _(s):
                s.activation(sqn[:, :], qh[h], AF.Square)
        with nc.Block() as blk:
            @blk.tensor
            def _(t):
                mm_acc(t, ps_row[:, :], [(onec_f[:, :], sqn[:, :])], TOWN)
        with nc.Block() as blk:
            @blk.vector
            def _(v):
                if h < 16:
                    v.tensor_copy(qn2_sb[0:1, h * TOWN:(h + 1) * TOWN], ps_row[0:1, :])
                else:
                    v.reduce_max(kn2_sb[0:1, h - 16:h - 15], ps_row[0:1, :],
                                 axis=AX.X)
    with nc.Block() as blk:
        @blk.sync
        def _(sy):
            dma(sy, o_qn2, qn2_sb[:, :])
            dma(sy, o_kn2, kn2_sb[:, :])
            sy.wait_ge(dsem, p.dcnt)
    n_ctx.close()
    p.ctx.close()
    return nc


def pre0_inputs(full):
    x = np.asarray(full["L0_attn_x"])[0]
    maps = []
    for c in range(NCORES):
        idx = own_token_index(c)
        xT = f32c(x[idx].T)
        maps.append({
            "xT": xT,
            "Wq": f32c(full["L0_attn_Wq"]),
            "Wk": f32c(full["L0_attn_Wk"]),
            "Wv": f32c(full["L0_attn_Wv"]),
            "WA": f32c(full["L0_attn_pol_WA"]),
            "WB": f32c(full["L0_attn_pol_WB"]),
            "cos0": f32c(CONSTS["cos0"][:, idx]),
            "sin0": f32c(CONSTS["sin0"][:, idx]),
            "ones_col": CONSTS["ones_col"],
            "ones_row": CONSTS["ones_row"],
        })
    return maps


def run_pre0(full):
    nc = build_pre0()
    res = run_bass_kernel_spmd(nc, pre0_inputs(full), core_ids=list(range(NCORES)))
    return res.results



SQD0 = float(np.sqrt(128.0))


def build_flash0(dbg=False):
    p = P()
    nc = p.nc
    qT = p.inp("qT", [2048, TOWN], BF16)
    aT = p.inp("aT", [32, TOWN], BF16)
    qn2 = p.inp("qn2", [1, 16 * TOWN], F32)
    qgl = p.inp("qgl", [1, TOWN], F32)
    kn2 = p.inp("kn2", [1, 32], F32)
    k_job = p.inp("k_job", [3, 128, 4 * 4096], BF16)
    v_job = p.inp("v_job", [3, 128, 32 * 512], BF16)
    b_job = p.inp("b_job", [32, 24 * 512], BF16)
    msk_job = p.inp("msk_job", [128, 24 * 2048], BF16)
    kgl_job = p.inp("kgl_job", [128, 96], F32)
    pol = p.inp("pol", [1, 16], F32)
    pgate = p.inp("pgate", [1, 16], F32)
    gtp = p.inp("gtp", [1, 16], F32)
    Wo = p.inp("Wo", [D, D], F32)
    c_I = p.inp("I128", [128, 128], BF16)
    c_onec = p.inp("ones_col", [128, 1], BF16)

    o_attn = p.out("attn0T", [D, TOWN], F32)

    if dbg:
        onorm_dram = p.out("onorm_dram", [D, TOWN], BF16)
        oacc_dram = p.out("oacc_dram", [D, 512], F32)
        sums_dram = p.out("sums_dram", [16, 512], F32)
        crow_out = p.out("crow_out", [1, 16 * TOWN], F32)
        ucol_out = p.out("ucol_out", [128, 16 * 96], F32)
        gcol_out = p.out("gcol_out", [128, 32], F32)
        msig_out = p.out("msig_out", [128, 32 * 512], BF16)
    else:
        onorm_dram = p.dram("onorm_dram", [D, TOWN], BF16)
        oacc_dram = p.dram("oacc_dram", [D, 512], F32)
        sums_dram = p.dram("sums_dram", [16, 512], F32)

    isb = p.sb("isb", [128, 128], BF16)
    onecb = p.sb("onecb", [128, 1], BF16)
    onesr_f = p.sb("onesr_f", [1, 128], F32)
    kglb = p.sb("kglb", [128, 96], F32)
    u_all = p.sb("u_all", [128, 16 * 96], F32)
    gI_all = p.sb("gI", [128, 16 * 128], BF16)
    prow = p.sb("prow", [1, 16 * 8], F32)
    kn2sb = p.sb("kn2sb", [1, 32], F32)
    gcol = p.sb("gcol", [128, 32], F32)
    crow_dram = p.dram("crow_dram", [1, 16 * TOWN], F32)

    dsem = p.sem("dsem")
    mtsem = p.sem("mtsem")
    sgsem = p.sem("sgsem")
    vmsem = p.sem("vmsem")
    tsem = p.sem("tsem")
    esem = p.sem("esem")
    vsem = p.sem("vsem")
    osem = p.sem("osem")
    stsem = p.sem("stsem")
    msksem = p.sem("msksem")
    wsem = p.sem("wsem")
    tsem2 = p.sem("tsem2")
    vsem2 = p.sem("vsem2")

    def dma(eng, out, in_):
        eng.dma_start(out=out, in_=in_).then_inc(dsem, 16)
        p.dcnt += 16

    def row(i):
        return prow[0:1, 16 * i:16 * (i + 1)]

    pp_ctx = ExitStack()
    qn2sb = p.sb("qn2sb", [1, 16 * TOWN], F32, pp_ctx)
    qglb = p.sb("qglb", [1, TOWN], F32, pp_ctx)
    rtmp = p.sb("rtmp", [1, TOWN], F32, pp_ctx)
    crow = p.sb("crow", [1, 16 * TOWN], F32, pp_ctx)
    with nc.Block() as blk:
        @blk.sync
        def _(sy):
            dma(sy, isb[:, :], c_I)
            dma(sy, onecb[:, :], c_onec)
            dma(sy, kglb[:, :], kgl_job)
            dma(sy, qn2sb[:, :], qn2)
            dma(sy, qglb[:, :], qgl)
            dma(sy, kn2sb[:, :], kn2)
            dma(sy, row(0), pol)
            dma(sy, row(1), pgate)
            dma(sy, row(2), gtp)
            sy.wait_ge(dsem, p.dcnt)

        @blk.vector
        def _(v):
            v.memset(onesr_f[:, :], 1.0)
    vself = p.sem("vself")
    vself_n = [0]

    def vc(instr):
        vself_n[0] += 1
        instr.then_inc(vself, 1)
        return vself_n[0]

    with nc.Block() as blk:
        @blk.vector
        def _(v):
            v.wait_ge(vself, vc(v.tensor_single_scalar(row(3), row(0), -1.0,
                                                       ALU.max)))
            v.wait_ge(vself, vc(v.tensor_single_scalar(row(3), row(3), 1.0,
                                                       ALU.min)))
            v.tensor_single_scalar(row(3), row(3), 1.0 / 4096.0, ALU.mult)
    with nc.Block() as blk:
        @blk.scalar
        def _(s):
            s.activation(row(4), row(2), AF.Abs)
    with nc.Block() as blk:
        @blk.scalar
        def _(s):
            s.activation(row(5), row(4), AF.Exp, scale=-1.0)
    with nc.Block() as blk:
        @blk.vector
        def _(v):
            v.tensor_single_scalar(row(5), row(5), 1.0, ALU.add)
    with nc.Block() as blk:
        @blk.scalar
        def _(s):
            s.activation(row(5), row(5), AF.Ln)
    with nc.Block() as blk:
        @blk.vector
        def _(v):
            v.wait_ge(vself, vc(v.tensor_single_scalar(row(6), row(2), 0.0,
                                                       ALU.max)))
            v.wait_ge(vself, vc(v.tensor_tensor(row(6), row(6), row(5),
                                                ALU.add)))
            v.wait_ge(vself, vc(v.tensor_single_scalar(row(6), row(6), 1e-6,
                                                       ALU.max)))
            v.tensor_tensor(row(7), row(3), row(6), ALU.add)
    with nc.Block() as blk:
        @blk.scalar
        def _(s):
            s.activation(row(4), row(1), AF.Sigmoid)
    with nc.Block() as blk:
        @blk.vector
        def _(v):
            v.wait_ge(vself, vc(v.tensor_single_scalar(row(4), row(4), 1e-6,
                                                       ALU.max)))
            v.tensor_single_scalar(row(4), row(4), SQD0, ALU.mult)
            v.tensor_single_scalar(row(5), row(7), SQD0, ALU.mult)
    pc_ctx = ExitStack()
    ps_c = p.ps("ps_c", [128, 32], ctx=pc_ctx)
    with nc.Block() as blk:
        @blk.tensor
        def _(t):
            t.matmul(ps_c[:, 0:16], onesr_f[:, :], row(7), start=True, stop=True)
            t.matmul(ps_c[:, 16:32], onesr_f[:, :], row(4), start=True, stop=True)
    with nc.Block() as blk:
        @blk.vector
        def _(v):
            v.tensor_copy(gcol[:, :], ps_c[:, :])
    pc_ctx.close()
    with nc.Block() as blk:
        @blk.vector
        def _(v):
            for h in range(16):
                v.tensor_scalar(u_all[:, 96 * h:96 * (h + 1)], kglb[:, :],
                                gcol[:, h:h + 1], None, ALU.mult)
            for h in range(16):
                v.tensor_scalar(gI_all[:, 128 * h:128 * (h + 1)], isb[:, :],
                                gcol[:, 16 + h:17 + h], None, ALU.mult)
    with nc.Block() as blk:
        @blk.vector
        def _(v):
            for kv in range(4):
                v.reduce_max(prow[0:1, 112 + kv:113 + kv],
                             kn2sb[0:1, 8 * kv:8 * (kv + 1)], axis=AX.X)
    with nc.Block() as blk:
        @blk.vector
        def _(v):
            for h in range(16):
                v.tensor_scalar(crow[0:1, TOWN * h:TOWN * (h + 1)],
                                qn2sb[0:1, TOWN * h:TOWN * (h + 1)],
                                prow[0:1, 112 + h // 4:113 + h // 4], None,
                                ALU.mult)
    with nc.Block() as blk:
        @blk.scalar
        def _(s):
            for h in range(16):
                s.activation(crow[0:1, TOWN * h:TOWN * (h + 1)],
                             crow[0:1, TOWN * h:TOWN * (h + 1)], AF.Sqrt)
    with nc.Block() as blk:
        @blk.vector
        def _(v):
            for h in range(16):
                sl = crow[0:1, TOWN * h:TOWN * (h + 1)]
                v.tensor_scalar(rtmp[0:1, :], qglb[0:1, :],
                                prow[0:1, 80 + h:81 + h], None, ALU.mult)
                v.tensor_tensor(sl, sl, rtmp[0:1, :], ALU.add)
                v.tensor_scalar(sl, sl, -1.0, -5.0 * SQD0, ALU.mult, ALU.add)
    with nc.Block() as blk:
        @blk.sync
        def _(sy):
            dma(sy, crow_dram, crow[0:1, :])
            sy.wait_ge(dsem, p.dcnt)
    pp_ctx.close()

    fa_ctx = ExitStack()
    crow_ph = p.sb("crow_ph", [1, 16 * 512], F32, fa_ctx)
    qsb_all = p.sb("qsb", [128, 16 * 512], BF16, fa_ctx)
    bTsb = p.sb("bTsb", [32, 8 * 512], BF16, fa_ctx)
    ksb = p.sb("ksb", [128, 4 * 4096], BF16, fa_ctx)
    vsb = p.sb("vsb", [128, 32 * 512], BF16, fa_ctx)
    msig_all = p.sb("msig", [128, 32 * 512], BF16, fa_ctx)
    mskrun = p.sb("mskrun", [128, 2 * 2048], BF16, fa_ctx)
    Pt_all = p.sb("Pt", [128, 2 * 512], BF16, fa_ctx)
    ob32 = p.sb("ob32", [128, 512], F32, fa_ctx)
    srow = p.sb("srow", [1, 512], F32, fa_ctx)
    mp = [p.ps(f"mp{i}", [128, 512], ctx=fa_ctx) for i in range(2)]
    sps = [p.ps(f"sps{i}", [128, 512], ctx=fa_ctx) for i in range(2)]
    outp = p.ps("outp", [128, 512], ctx=fa_ctx)
    sump = p.ps("sump", [1, 512], ctx=fa_ctx)
    qsb = [qsb_all[:, i * 512:(i + 1) * 512] for i in range(16)]
    msig = [msig_all[:, 512 * i:512 * (i + 1)] for i in range(32)]
    Pt = [Pt_all[:, 0:512], Pt_all[:, 512:1024]]

    ctr = dict(mt=0, sg=0, vm=0, t=0, e=0, v=0, msk=0, o=0)

    def emit_phase(ph):
        qc0 = 0 if ph == 0 else 512
        mt0, sg0, vm0 = ctr["mt"], ctr["sg"], ctr["vm"]
        t0, e0, v0, msk0, o0 = ctr["t"], ctr["e"], ctr["v"], ctr["msk"], ctr["o"]
        accum = (ph == 2)
        with nc.Block() as blk:
            @blk.sync
            def _(sy):
                dma(sy, ksb[:, :], k_job[ph])
                dma(sy, vsb[:, :], v_job[ph])
                dma(sy, bTsb[:, :], b_job[:, (ph * 8) * 512:(ph * 8 + 8) * 512])
                for i in range(16):
                    dma(sy, qsb[i], qT[128 * i:128 * (i + 1), qc0:qc0 + 512])
                for h in range(16):
                    dma(sy, crow_ph[0:1, 512 * h:512 * (h + 1)],
                        crow_dram[0:1, TOWN * h + qc0:TOWN * h + qc0 + 512])
                for job in range(8):
                    if job >= 2:
                        sy.wait_ge(vmsem, vm0 + job - 1)
                    sy.dma_start(out=mskrun[:, 2048 * (job % 2):2048 * (job % 2 + 1)],
                                 in_=msk_job[:, (ph * 8 + job) * 2048:
                                             (ph * 8 + job + 1) * 2048]
                                 ).then_inc(msksem, 16)

            @blk.tensor
            def _(t):
                t.wait_ge(dsem, p.dcnt)
                for ch in range(32):
                    if ch >= 2:
                        t.wait_ge(sgsem, sg0 + ch - 1)
                    job, cc = ch // 4, ch % 4
                    t.matmul(mp[ch % 2][:, :],
                             bTsb[:, job * 512 + 128 * cc:job * 512 + 128 * (cc + 1)],
                             absb[:, qc0:qc0 + 512],
                             start=True, stop=True).then_inc(mtsem, 1)
                t.wait_ge(vmsem, vm0 + 8)
                for h in range(16):
                    kv = h // 4
                    for ch in range(33):
                        if ch >= 1:
                            t.wait_ge(esem, e0 + 32 * h + ch)
                            if ch == 1:
                                t.wait_ge(vsem, v0 + h)
                            pc = ch - 1
                            t.matmul(outp[:, :],
                                     vsb[:, 512 * pc + 128 * kv:512 * pc + 128 * (kv + 1)],
                                     Pt[pc % 2][:, :],
                                     start=(pc == 0), stop=(pc == 31))
                            mm = t.matmul(sump[0:1, :], onecb[:, :], Pt[pc % 2][:, :],
                                          start=(pc == 0), stop=(pc == 31))
                            if ch == 32:
                                mm.then_inc(tsem, 1)
                        if ch < 32:
                            sp = sps[ch % 2]
                            t.matmul(sp[:, :], onesr_f[:, :],
                                     crow_ph[0:1, 512 * h:512 * (h + 1)],
                                     start=True, stop=False)
                            t.matmul(sp[:, :],
                                     ksb[:, 4096 * kv + 128 * ch:4096 * kv + 128 * (ch + 1)],
                                     qsb[h][:, :],
                                     start=False, stop=False)
                            t.matmul(sp[:, :], gI_all[:, 128 * h:128 * (h + 1)],
                                     msig[ch],
                                     start=False, stop=True).then_inc(tsem, 1)

            @blk.scalar
            def _(s):
                for ch in range(32):
                    s.wait_ge(mtsem, mt0 + ch + 1)
                    s.activation(msig[ch], mp[ch % 2][:, :],
                                 AF.Sigmoid).then_inc(sgsem, 1)
                for h in range(16):
                    for ch in range(32):
                        s.wait_ge(tsem, t0 + 33 * h + ch + 1)
                        s.activation(Pt[ch % 2][:, :], sps[ch % 2][:, :], AF.Exp,
                                     scale=1.0 / SQD0,
                                     bias=u_all[:, 96 * h + ph * 32 + ch:
                                                96 * h + ph * 32 + ch + 1]
                                     ).then_inc(esem, 1)

            @blk.vector
            def _(v):
                for job in range(8):
                    v.wait_ge(msksem, msk0 + 16 * (job + 1))
                    v.wait_ge(sgsem, sg0 + 4 * (job + 1))
                    for cc in range(4):
                        ch = job * 4 + cc
                        tt = v.tensor_tensor(msig[ch], msig[ch],
                                             mskrun[:, 2048 * (job % 2) + 512 * cc:
                                                    2048 * (job % 2) + 512 * (cc + 1)],
                                             ALU.add)
                    tt.then_inc(vmsem, 1)
                for h in range(16):
                    v.wait_ge(tsem, t0 + 33 * (h + 1))
                    if h >= 1:
                        v.wait_ge(osem, o0 + 32 * h)
                    v.tensor_copy(ob32[:, :], outp[:, :])
                    tt = v.tensor_copy(srow[0:1, :], sump[0:1, :])
                    tt.then_inc(vsem, 1)

            @blk.gpsimd
            def _(g):
                for h in range(16):
                    g.wait_ge(vsem, v0 + h + 1)
                    op = ALU.add if accum else ALU.bypass
                    g.dma_start(out=oacc_dram[128 * h:128 * (h + 1), :],
                                in_=ob32[:, :], accum_op=op).then_inc(osem, 16)
                    g.dma_start(out=sums_dram[h:h + 1, :],
                                in_=srow[0:1, :], accum_op=op).then_inc(osem, 16)
        ctr["mt"] += 32
        ctr["msk"] += 16 * 8
        ctr["sg"] += 32
        ctr["vm"] += 8
        ctr["t"] += 33 * 16
        ctr["e"] += 32 * 16
        ctr["v"] += 16
        ctr["o"] += 32 * 16

    def emit_normalize(ph):
        qc0 = 0 if ph == 0 else 512
        nn_ctx = ExitStack()
        of32 = p.sb("of32_%d" % ph, [128, 512], F32, nn_ctx)
        nrow = p.sb("nrow_%d" % ph, [1, 512], F32, nn_ctx)
        nbf = p.sb("nbf_%d" % ph, [128, 512], BF16, nn_ctx)
        ps_n = p.ps("ps_n_%d" % ph, [128, 512], ctx=nn_ctx)
        with nc.Block() as blk:
            @blk.sync
            def _(sy):
                sy.wait_ge(osem, ctr["o"])
        for h in range(16):
            with nc.Block() as blk:
                @blk.sync
                def _(sy):
                    dma(sy, of32[:, :], oacc_dram[128 * h:128 * (h + 1), :])
                    dma(sy, nrow[0:1, :], sums_dram[h:h + 1, :])
                    sy.wait_ge(dsem, p.dcnt)
            with nc.Block() as blk:
                @blk.vector
                def _(v):
                    v.reciprocal(nrow[0:1, :], nrow[0:1, :])
            with nc.Block() as blk:
                @blk.tensor
                def _(t):
                    t.matmul(ps_n[:, :], onesr_f[:, :], nrow[0:1, :],
                             start=True, stop=True)
            with nc.Block() as blk:
                @blk.vector
                def _(v):
                    v.tensor_tensor(nbf[:, :], of32[:, :], ps_n[:, :], ALU.mult)
            with nc.Block() as blk:
                @blk.sync
                def _(sy):
                    dma(sy, onorm_dram[128 * h:128 * (h + 1), qc0:qc0 + 512],
                        nbf[:, :])
                    sy.wait_ge(dsem, p.dcnt)
        nn_ctx.close()

    absb = p.sb("absb", [32, TOWN], BF16, fa_ctx)
    with nc.Block() as blk:
        @blk.sync
        def _(sy):
            dma(sy, absb[:, :], aT)
            sy.wait_ge(dsem, p.dcnt)

    if dbg:
        with nc.Block() as blk:
            @blk.sync
            def _(sy):
                dma(sy, crow_out, crow_dram)
                dma(sy, ucol_out, u_all[:, :])
                dma(sy, gcol_out, gcol[:, :])
                sy.wait_ge(dsem, p.dcnt)
    emit_phase(0)
    if dbg:
        with nc.Block() as blk:
            @blk.sync
            def _(sy):
                dma(sy, msig_out, msig_all[:, :])
                sy.wait_ge(dsem, p.dcnt)
    emit_normalize(0)
    emit_phase(1)
    emit_phase(2)
    emit_normalize(2)
    fa_ctx.close()

    wo_ctx = ExitStack()
    onsb_all = p.sb("onsb", [128, 16 * TOWN], BF16, wo_ctx)
    wos = [p.sb(f"wos{i}", [128, 16, 128], BF16, wo_ctx) for i in range(2)]
    obuf2 = [p.sb(f"obuf2_{i}", [128, TOWN], F32, wo_ctx) for i in range(2)]
    wop = [p.ps(f"wop{i}", [128, TOWN], ctx=wo_ctx) for i in range(2)]
    onsb = [onsb_all[:, i * TOWN:(i + 1) * TOWN] for i in range(16)]
    with nc.Block() as blk:
        @blk.sync
        def _(sy):
            for i in range(16):
                dma(sy, onsb[i], onorm_dram[128 * i:128 * (i + 1), :])
            sy.wait_ge(dsem, p.dcnt)
    with nc.Block() as blk:
        @blk.gpsimd
        def _(g):
            for o in range(16):
                if o >= 2:
                    g.wait_ge(tsem2, o - 1)
                g.dma_start(out=wos[o % 2][:, :, :],
                            in_=Wo[:, 128 * o:128 * (o + 1)].rearrange(
                                "(hh p) n -> p hh n", p=128)
                            ).then_inc(wsem, 16)

        @blk.tensor
        def _(t):
            for o in range(16):
                t.wait_ge(wsem, 16 * (o + 1))
                if o >= 2:
                    t.wait_ge(vsem2, o - 1)
                mm = mm_acc(t, wop[o % 2][:, :],
                            [(wos[o % 2][:, i, :], onsb[i])
                             for i in range(16)], TOWN)
                mm.then_inc(tsem2, 1)

        @blk.vector
        def _(v):
            for o in range(16):
                v.wait_ge(tsem2, o + 1)
                if o >= 2:
                    v.wait_ge(stsem, 16 * (o - 1))
                v.tensor_copy(obuf2[o % 2][:, :],
                              wop[o % 2][:, :]).then_inc(vsem2, 1)

        @blk.sync
        def _(sy):
            for o in range(16):
                sy.wait_ge(vsem2, o + 1)
                sy.dma_start(out=o_attn[128 * o:128 * (o + 1), :],
                             in_=obuf2[o % 2][:, :]).then_inc(stsem, 16)
            sy.wait_ge(stsem, 16 * 16)
    wo_ctx.close()
    p.ctx.close()
    return nc


def flash0_inputs(full, pre):
    C = NCORES
    kT_full = np.concatenate([np.asarray(pre[c]["kT"]) for c in range(C)], axis=1)
    v_full = np.concatenate([np.asarray(pre[c]["v"]) for c in range(C)], axis=0)
    bT_full = np.concatenate([np.asarray(pre[c]["bT"]) for c in range(C)], axis=1)
    kn2_full = np.stack([np.asarray(pre[c]["kn2"]).reshape(4) for c in range(C)],
                        axis=1).reshape(1, 32)
    bf = ml_dtypes.bfloat16
    maps = []
    for c in range(C):
        runsA = list(range(0, c + 1)) + [None] * (7 - c)
        runsB = (list(range(0, 16 - c)) + [None] * 16)[:16]
        jobs = runsA + runsB
        k_job = np.zeros((3, 128, 4 * 4096), dtype=bf)
        v_job = np.zeros((3, 128, 32 * 512), dtype=bf)
        b_job = np.zeros((32, 24 * 512), dtype=bf)
        msk = np.zeros((128, 24 * 2048), dtype=np.float32)
        kgl = np.zeros((128, 96), dtype=np.float32)
        for j, g in enumerate(jobs):
            ph, jp = j // 8, j % 8
            if g is None:
                msk[:, j * 2048:(j + 1) * 2048] = -1e30
                continue
            off = run_owner_offset(g)
            ksl = slice(off, off + 512)
            kt = kT_full[:, ksl].reshape(4, 128, 512)
            for kv in range(4):
                k_job[ph, :, kv * 4096 + jp * 512: kv * 4096 + (jp + 1) * 512] = kt[kv]
            vt = v_full[ksl].reshape(4, 128, 512)
            for ks in range(4):
                v_job[ph, :, (jp * 4 + ks) * 512:(jp * 4 + ks + 1) * 512] = vt[ks]
            b_job[:, j * 512:(j + 1) * 512] = bT_full[:, ksl]
            kgl[:, 4 * j:4 * (j + 1)] = KGLOB[off:off + 512].reshape(4, 128).T
            own_g = c if j < 8 else 15 - c
            if g == own_g:
                pp = np.arange(128)[:, None]
                ff = np.arange(512)[None, :]
                for cc in range(4):
                    m = np.where(128 * cc + pp > ff, -1e30, 0.0)
                    msk[:, j * 2048 + 512 * cc: j * 2048 + 512 * (cc + 1)] = m
        idx = own_token_index(c)
        maps.append({
            "qT": np.asarray(pre[c]["qT"]),
            "aT": np.asarray(pre[c]["aT"]),
            "qn2": np.asarray(pre[c]["qn2"]),
            "qgl": f32c(idx.reshape(1, TOWN)),
            "kn2": kn2_full,
            "k_job": k_job, "v_job": v_job, "b_job": b_job,
            "msk_job": msk.astype(bf), "kgl_job": kgl,
            "pol": f32c(full["L0_attn_pol_dir"].reshape(1, 16)),
            "pgate": f32c(full["L0_attn_pol_gate"].reshape(1, 16)),
            "gtp": f32c(full["L0_attn_gtp_gamma"].reshape(1, 16)),
            "Wo": f32c(full["L0_attn_Wo"]),
            "I128": CONSTS["I128"],
            "ones_col": CONSTS["ones_col"],
        })
    return maps


def run_flash0(full, pre, dbg=False):
    nc = build_flash0(dbg=dbg)
    res = run_bass_kernel_spmd(nc, flash0_inputs(full, pre),
                               core_ids=list(range(NCORES)))
    return res.results



def build_dequant0():
    p = P()
    nc = p.nc
    srcs = []
    for nm in ("gate", "up", "down"):
        pt = p.inp(f"{nm}_pT", [1024, 2048], I32)
        st = p.inp(f"{nm}_sT", [64, 2048], F32)
        ow = p.out(f"{nm}_w", [2048, 2048], BF16)
        srcs.append((nm, pt, st, ow))
    c_e8 = p.inp("E8", [8, 128], F32)

    e8 = p.sb("e8", [8, 128], F32)
    bm2 = p.sb("bm2", [128, 1], F32)
    bm6 = p.sb("bm6", [128, 1], F32)
    pt_sb = p.sb("pt_sb", [128, 2048], I32)
    st_sb = p.sb("st_sb", [8, 2048], F32)
    hi = p.sb("hi", [128, 2048], I32)
    r_lo = p.sb("r_lo", [128, 2048], F32)
    r_hi = p.sb("r_hi", [128, 2048], F32)
    sg_lo = p.sb("sg_lo", [128, 2048], F32)
    sg_hi = p.sb("sg_hi", [128, 2048], F32)
    it1 = p.sb("it1", [128, 2048], I32)
    it2 = p.sb("it2", [128, 2048], I32)
    rel1 = p.sb("rel1", [128, 2048], F32)
    rel2 = p.sb("rel2", [128, 2048], F32)
    mtmp = p.sb("mtmp", [128, 2048], F32)
    scl = p.sb("scl", [128, 2048], F32)
    wout = p.sb("wout", [128, 2048], BF16)
    ps_sc = p.ps("ps_sc", [128, 2048])
    dsem = p.sem("dsem")

    def dma(eng, out, in_):
        eng.dma_start(out=out, in_=in_).then_inc(dsem, 16)
        p.dcnt += 16

    with nc.Block() as blk:
        @blk.sync
        def _(sy):
            dma(sy, e8[:, :], c_e8)

        @blk.vector
        def _(v):
            v.memset(bm2[:, :], -2.0)
            v.memset(bm6[:, :], -6.0)

    for nm, pt, st, ow in srcs:
        for s in range(8):
            with nc.Block() as blk:
                @blk.sync
                def _(sy):
                    dma(sy, pt_sb[:, :], pt[128 * s:128 * (s + 1), :])
                    dma(sy, st_sb[:, :], st[8 * s:8 * (s + 1), :])
                    sy.wait_ge(dsem, p.dcnt)
            with nc.Block() as blk:
                @blk.tensor
                def _(t):
                    mm_acc(t, ps_sc[:, :], [(e8[:, :], st_sb[:, :])], 2048)
            with nc.Block() as blk:
                @blk.vector
                def _(v):
                    v.tensor_copy(scl[:, :], ps_sc[:, :])

                @blk.vector
                def _(g):
                    g.tensor_single_scalar(hi[:, :], pt_sb[:, :], 4,
                                           ALU.logical_shift_right)
                    g.tensor_single_scalar(it1[:, :], pt_sb[:, :], 7,
                                           ALU.bitwise_and)
                    g.tensor_copy(r_lo[:, :], it1[:, :])
                    g.tensor_single_scalar(it2[:, :], pt_sb[:, :], 8,
                                           ALU.bitwise_and)
                    g.tensor_copy(sg_lo[:, :], it2[:, :])
            with nc.Block() as blk:
                @blk.vector
                def _(g):
                    g.tensor_single_scalar(it1[:, :], hi[:, :], 7, ALU.bitwise_and)
                    g.tensor_copy(r_hi[:, :], it1[:, :])
                    g.tensor_single_scalar(it2[:, :], hi[:, :], 8, ALU.bitwise_and)
                    g.tensor_copy(sg_hi[:, :], it2[:, :])
            for which, rr, sg, rowbase in (("lo", r_lo, sg_lo, 256 * s),
                                           ("hi", r_hi, sg_hi, 256 * s + 128)):
                with nc.Block() as blk:
                    @blk.scalar
                    def _(sc):
                        sc.activation(rel1[:, :], rr[:, :], AF.Relu, scale=0.5,
                                      bias=bm2[:, 0:1])
                        sc.activation(rel2[:, :], rr[:, :], AF.Relu,
                                      bias=bm6[:, 0:1])

                    @blk.vector
                    def _(v):
                        v.tensor_scalar(sg[:, :], sg[:, :], -0.25, 1.0,
                                        ALU.mult, ALU.add)
                with nc.Block() as blk:
                    @blk.vector
                    def _(v):
                        v.scalar_tensor_tensor(mtmp[:, :], rr[:, :], 0.5,
                                               rel1[:, :], ALU.mult, ALU.add)
                        v.tensor_tensor(mtmp[:, :], mtmp[:, :], rel2[:, :],
                                        ALU.add)
                        v.tensor_tensor(mtmp[:, :], mtmp[:, :], sg[:, :],
                                        ALU.mult)
                        v.tensor_tensor(wout[:, :], mtmp[:, :], scl[:, :],
                                        ALU.mult)
                with nc.Block() as blk:
                    @blk.sync
                    def _(sy):
                        dma(sy, ow[rowbase:rowbase + 128, :], wout[:, :])
                        sy.wait_ge(dsem, p.dcnt)
    p.ctx.close()
    return nc


PERM_N16 = perm256(16384)


def dequant0_inputs(full):
    gp = np.asarray(full["L0_ffn_gate_W_packed"]).T[:, PERM_N16]
    up = np.asarray(full["L0_ffn_up_W_packed"]).T[:, PERM_N16]
    gs = np.ascontiguousarray(np.asarray(full["L0_ffn_gate_scales"]).T[:, PERM_N16])
    us = np.ascontiguousarray(np.asarray(full["L0_ffn_up_scales"]).T[:, PERM_N16])
    dp = np.asarray(full["L0_ffn_down_W_packed"]).T
    ds = np.asarray(full["L0_ffn_down_scales"]).T
    maps = []
    for c in range(NCORES):
        nsl = slice(2048 * c, 2048 * (c + 1))
        maps.append({
            "gate_pT": np.ascontiguousarray(gp[:, nsl], np.int32),
            "gate_sT": f32c(gs[:, nsl]),
            "up_pT": np.ascontiguousarray(up[:, nsl], np.int32),
            "up_sT": f32c(us[:, nsl]),
            "down_pT": np.ascontiguousarray(dp[1024 * c:1024 * (c + 1), :], np.int32),
            "down_sT": f32c(ds[64 * c:64 * (c + 1), :]),
            "E8": CONSTS["E8"],
        })
    return maps


def run_dequant0(full):
    nc = build_dequant0()
    res = run_bass_kernel_spmd(nc, dequant0_inputs(full),
                               core_ids=list(range(NCORES)))
    return res.results



GELU_C = 0.044715 * 0.7978845608028654


def build_ffn0(ngrp=8, do_down=True, serial=False):
    p = P()
    nc = p.nc
    xnT = p.inp("xnT", [D, TOWN], BF16)
    xT = p.inp("xT", [D, TOWN], F32)
    attn0T = p.inp("attn0T", [D, TOWN], F32)
    gW = p.inp("gW", [D, 16384], BF16)
    uW = p.inp("uW", [D, 16384], BF16)
    dW = p.inp("dW", [16384, D], BF16)
    cP = [p.inp(nm, [128, 128], BF16) for nm in ("PEe", "POe", "PEo", "POo")]

    o_x1 = p.out("x1T", [D, TOWN], F32)

    xnp_all = p.sb("xnp", [128, 16 * TOWN], BF16)
    xnp = [xnp_all[:, i * TOWN:(i + 1) * TOWN] for i in range(16)]
    inter_all = p.sb("inter", [128, 16 * TOWN], BF16)
    inter = [inter_all[:, i * TOWN:(i + 1) * TOWN] for i in range(16)]
    perm = p.sb("perm", [128, 4 * 128], BF16)
    g2 = p.sb("g2", [128, 2 * TOWN], F32)
    arg = p.sb("arg", [128, 2 * TOWN], F32)
    th = p.sb("th", [128, 2 * TOWN], F32)
    gsc = p.sb("gsc", [128, TOWN], F32)
    g3 = p.sb("g3", [128, TOWN], F32)
    half = p.sb("half", [128, TOWN], F32)
    gl = p.sb("gl", [128, TOWN], F32)
    upf = p.sb("upf", [128, TOWN], F32)
    ob32 = p.sb("ob32_", [128, 2 * TOWN], F32)
    init_ctx = ExitStack()
    f32a = p.sb("f32a", [128, TOWN], F32, init_ctx)
    f32b = p.sb("f32b", [128, TOWN], F32, init_ctx)
    psum4 = [p.ps("psum4_%d" % i, [128, TOWN]) for i in range(4)]
    dsem = p.sem("dsem")
    tsem = p.sem("tsem")
    vsem = p.sem("vsem")
    s1sem = p.sem("s1sem")
    s2sem = p.sem("s2sem")
    v1sem = p.sem("v1sem")
    v2sem = p.sem("v2sem")
    dtsem = p.sem("dtsem")
    dvsem = p.sem("dvsem")
    osem = p.sem("osem")
    ldsem = p.sem("ldsem")

    def dma(eng, out, in_):
        eng.dma_start(out=out, in_=in_).then_inc(dsem, 16)
        p.dcnt += 16

    for ch in range(16):
        with nc.Block() as blk:
            @blk.sync
            def _(sy):
                dma(sy, f32a[:, :], xT[128 * ch:128 * (ch + 1), :])
                dma(sy, f32b[:, :], attn0T[128 * ch:128 * (ch + 1), :])
                sy.wait_ge(dsem, p.dcnt)
        with nc.Block() as blk:
            @blk.vector
            def _(v):
                v.tensor_tensor(ob32[:, 0:TOWN], f32a[:, :], f32b[:, :], ALU.add)
        with nc.Block() as blk:
            @blk.sync
            def _(sy):
                dma(sy, o_x1[128 * ch:128 * (ch + 1), :], ob32[:, 0:TOWN])
                sy.wait_ge(dsem, p.dcnt)

    init_ctx.close()

    pm_ctx = ExitStack()
    xn_all = p.sb("xn_all", [128, 16 * TOWN], BF16, pm_ctx)
    ps_pm = [psum4[0], psum4[1]]
    xn = [xn_all[:, i * TOWN:(i + 1) * TOWN] for i in range(16)]
    with nc.Block() as blk:
        @blk.sync
        def _(sy):
            for i in range(16):
                dma(sy, xn[i], xnT[128 * i:128 * (i + 1), :])
            for j in range(4):
                dma(sy, perm[:, 128 * j:128 * (j + 1)], cP[j])
            sy.wait_ge(dsem, p.dcnt)
    with nc.Block() as blk:
        @blk.tensor
        def _(t):
            for dch in range(16):
                if dch >= 2:
                    t.wait_ge(vsem, dch - 1)
                s2 = dch // 2
                PA = perm[:, 0:128] if dch % 2 == 0 else perm[:, 256:384]
                PB = perm[:, 128:256] if dch % 2 == 0 else perm[:, 384:512]
                mm = mm_acc(t, ps_pm[dch % 2][:, :],
                            [(PA, xn[2 * s2]), (PB, xn[2 * s2 + 1])], TOWN)
                mm.then_inc(tsem, 1)

        @blk.vector
        def _(v):
            for dch in range(16):
                v.wait_ge(tsem, dch + 1)
                v.tensor_copy(xnp[dch], ps_pm[dch % 2][:, :]).then_inc(vsem, 1)
    pm_ctx.close()

    ctr = dict(t=16, s1=0, v1=0, s2=0, v2=0, dt=0, dv=0, o=0, ld=0)
    for grp in range(ngrp):
        for hf in range(2):
            gu_ctx = ExitStack()
            wg = p.sb("wg%d_%d" % (grp, hf), [128, 16, 1024], BF16, gu_ctx)
            wu = p.sb("wu%d_%d" % (grp, hf), [128, 16, 1024], BF16, gu_ctx)
            gps = [psum4[0], psum4[1]]
            ups = [psum4[2], psum4[3]]
            t0, s10, v10 = ctr["t"], ctr["s1"], ctr["v1"]
            s20, v20 = ctr["s2"], ctr["v2"]
            nb = 2048 * grp + 1024 * hf
            with nc.Block() as blk:
                @blk.sync
                def _(sy):
                    for i in range(16):
                        dma(sy, wg[:, i, :], gW[128 * i:128 * (i + 1), nb:nb + 1024])
                        dma(sy, wu[:, i, :], uW[128 * i:128 * (i + 1), nb:nb + 1024])
                    sy.wait_ge(dsem, p.dcnt)
            if serial:
                for ncx in range(8):
                    b = slice(TOWN * (ncx % 2), TOWN * (ncx % 2 + 1))
                    csl = slice(128 * ncx, 128 * (ncx + 1))
                    with nc.Block() as blk:
                        @blk.tensor
                        def _(t):
                            mm_acc(t, gps[ncx % 2][:, :],
                                   [(wg[:, i, csl], xnp[i]) for i in range(16)],
                                   TOWN)
                            mm_acc(t, ups[ncx % 2][:, :],
                                   [(wu[:, i, csl], xnp[i]) for i in range(16)],
                                   TOWN)
                    with nc.Block() as blk:
                        @blk.scalar
                        def _(s):
                            s.activation(g2[:, b], gps[ncx % 2][:, :], AF.Square)
                    with nc.Block() as blk:
                        @blk.vector
                        def _(v):
                            v.tensor_scalar(gsc[:, :], gps[ncx % 2][:, :],
                                            0.7978845608028654, None, ALU.mult)
                            v.tensor_tensor(g3[:, :], g2[:, b], gps[ncx % 2][:, :],
                                            ALU.mult)
                            v.scalar_tensor_tensor(arg[:, b], g3[:, :], GELU_C,
                                                   gsc[:, :], ALU.mult, ALU.add)
                    with nc.Block() as blk:
                        @blk.scalar
                        def _(s):
                            s.activation(th[:, b], arg[:, b], AF.Tanh)
                    with nc.Block() as blk:
                        @blk.vector
                        def _(v):
                            v.tensor_scalar(half[:, :], th[:, b], 0.5, 0.5,
                                            ALU.mult, ALU.add)
                            v.tensor_tensor(gl[:, :], half[:, :],
                                            gps[ncx % 2][:, :], ALU.mult)
                            v.tensor_copy(upf[:, :], ups[ncx % 2][:, :])
                            v.tensor_tensor(inter[8 * hf + ncx], gl[:, :],
                                            upf[:, :], ALU.mult)
                gu_ctx.close()
                continue
            with nc.Block() as blk:
                @blk.tensor
                def _(t):
                    for ncx in range(8):
                        if ncx >= 2:
                            t.wait_ge(v2sem, v20 + ncx - 1)
                        csl = slice(128 * ncx, 128 * (ncx + 1))
                        mm_acc(t, gps[ncx % 2][:, :],
                               [(wg[:, i, csl], xnp[i]) for i in range(16)], TOWN)
                        mm = mm_acc(t, ups[ncx % 2][:, :],
                                    [(wu[:, i, csl], xnp[i]) for i in range(16)],
                                    TOWN)
                        mm.then_inc(tsem, 1)

                @blk.scalar
                def _(s):
                    for ncx in range(8):
                        b = slice(TOWN * (ncx % 2), TOWN * (ncx % 2 + 1))
                        s.wait_ge(tsem, t0 + ncx + 1)
                        s.activation(g2[:, b], gps[ncx % 2][:, :],
                                     AF.Square).then_inc(s1sem, 1)
                        s.wait_ge(v1sem, v10 + ncx + 1)
                        s.activation(th[:, b], arg[:, b],
                                     AF.Tanh).then_inc(s2sem, 1)

                @blk.vector
                def _(v):
                    for ncx in range(8):
                        b = slice(TOWN * (ncx % 2), TOWN * (ncx % 2 + 1))
                        v.wait_ge(s1sem, s10 + ncx + 1)
                        v.tensor_scalar(gsc[:, :], gps[ncx % 2][:, :],
                                        0.7978845608028654, None, ALU.mult)
                        v.tensor_tensor(g3[:, :], g2[:, b], gps[ncx % 2][:, :],
                                        ALU.mult)
                        v.scalar_tensor_tensor(arg[:, b], g3[:, :], GELU_C,
                                               gsc[:, :], ALU.mult,
                                               ALU.add).then_inc(v1sem, 1)
                        v.wait_ge(s2sem, s20 + ncx + 1)
                        v.tensor_scalar(half[:, :], th[:, b], 0.5, 0.5,
                                        ALU.mult, ALU.add)
                        v.tensor_tensor(gl[:, :], half[:, :], gps[ncx % 2][:, :],
                                        ALU.mult)
                        v.tensor_copy(upf[:, :], ups[ncx % 2][:, :])
                        v.tensor_tensor(inter[8 * hf + ncx], gl[:, :], upf[:, :],
                                        ALU.mult).then_inc(v2sem, 1)
            ctr["t"] += 8
            ctr["s1"] += 8
            ctr["v1"] += 8
            ctr["s2"] += 8
            ctr["v2"] += 8
            gu_ctx.close()

        if not do_down:
            continue
        dn_ctx = ExitStack()
        wd = p.sb("wd%d" % grp, [128, 16, 2048], BF16, dn_ctx)
        ldb = p.sb("ldb%d" % grp, [128, 2 * TOWN], F32, dn_ctx)
        dps = [psum4[0], psum4[1]]
        dt0, dv0, o0, ld0 = ctr["dt"], ctr["dv"], ctr["o"], ctr["ld"]
        with nc.Block() as blk:
            @blk.sync
            def _(sy):
                for i in range(16):
                    dma(sy, wd[:, i, :],
                        dW[2048 * grp + 128 * i:2048 * grp + 128 * (i + 1), :])
                sy.wait_ge(dsem, p.dcnt)
        with nc.Block() as blk:
            @blk.sync
            def _(sy):
                sy.wait_ge(osem, o0)
                for hch in range(16):
                    if hch >= 2:
                        sy.wait_ge(dvsem, dv0 + hch - 1)
                    b = slice(TOWN * (hch % 2), TOWN * (hch % 2 + 1))
                    sy.dma_start(out=ldb[:, b],
                                 in_=o_x1[128 * hch:128 * (hch + 1), :]
                                 ).then_inc(ldsem, 16)

            @blk.tensor
            def _(t):
                for hch in range(16):
                    if hch >= 2:
                        t.wait_ge(dvsem, dv0 + hch - 1)
                    hsl = slice(128 * hch, 128 * (hch + 1))
                    mm = mm_acc(t, dps[hch % 2][:, :],
                                [(wd[:, i, hsl], inter[i]) for i in range(16)],
                                TOWN)
                    mm.then_inc(dtsem, 1)

            @blk.vector
            def _(v):
                for hch in range(16):
                    v.wait_ge(dtsem, dt0 + hch + 1)
                    v.wait_ge(ldsem, ld0 + 16 * (hch + 1))
                    if hch >= 2:
                        v.wait_ge(osem, o0 + 16 * (hch - 1))
                    b = slice(TOWN * (hch % 2), TOWN * (hch % 2 + 1))
                    v.tensor_tensor(ob32[:, b], ldb[:, b], dps[hch % 2][:, :],
                                    ALU.add).then_inc(dvsem, 1)

            @blk.gpsimd
            def _(g):
                for hch in range(16):
                    g.wait_ge(dvsem, dv0 + hch + 1)
                    b = slice(TOWN * (hch % 2), TOWN * (hch % 2 + 1))
                    g.dma_start(out=o_x1[128 * hch:128 * (hch + 1), :],
                                in_=ob32[:, b]).then_inc(osem, 16)
        ctr["dt"] += 16
        ctr["dv"] += 16
        ctr["o"] += 256
        ctr["ld"] += 256
        dn_ctx.close()
    with nc.Block() as blk:
        @blk.sync
        def _(sy):
            sy.wait_ge(osem, ctr["o"])
    p.ctx.close()
    return nc


def ffn0_inputs(full, pre, fl, deq):
    gWf = np.concatenate([np.asarray(deq[c]["gate_w"]) for c in range(NCORES)],
                         axis=1)
    uWf = np.concatenate([np.asarray(deq[c]["up_w"]) for c in range(NCORES)],
                         axis=1)
    dWf = np.concatenate([np.asarray(deq[c]["down_w"]) for c in range(NCORES)],
                         axis=0)
    x = np.asarray(full["L0_attn_x"])[0]
    maps = []
    for c in range(NCORES):
        idx = own_token_index(c)
        maps.append({
            "xnT": np.asarray(pre[c]["xnT"]),
            "xT": f32c(x[idx].T),
            "attn0T": np.asarray(fl[c]["attn0T"]),
            "gW": gWf, "uW": uWf, "dW": dWf,
            "PEe": CONSTS["PEe"], "POe": CONSTS["POe"],
            "PEo": CONSTS["PEo"], "POo": CONSTS["POo"],
        })
    return maps


def run_ffn0(full, pre, fl, deq, ngrp=8, do_down=True, serial=False):
    nc = build_ffn0(ngrp=ngrp, do_down=do_down, serial=serial)
    res = run_bass_kernel_spmd(nc, ffn0_inputs(full, pre, fl, deq),
                               core_ids=list(range(NCORES)))
    return res.results



SQD1 = float(np.sqrt(192.0))
BIGR = 1.0e30


def build_pre1():
    p = P()
    nc = p.nc
    x1T = p.inp("x1T", [D, TOWN], F32)
    wqa = p.inp("wqa", [D, 1536], F32)
    wqbn = p.inp("wqbn", [1536, 2048], F32)
    wqbp = p.inp("wqbp", [1536, 1024], F32)
    qaln = p.inp("qaln", [128, 12], F32)
    wkva = p.inp("wkva", [D, 576], F32)
    kvln = p.inp("kvln", [128, 4], F32)
    wkvbn = p.inp("wkvbn", [512, 2048], F32)
    wkvbv = p.inp("wkvbv", [512, 2048], F32)
    gwT = p.inp("gwT", [D, 32], F32)
    gbias = p.inp("gbias", [1, 32], F32)
    fc1T = p.inp("fc1T", [D, 1024], F32)
    cos1 = p.inp("cos1", [64, TOWN], F32)
    sin1 = p.inp("sin1", [64, TOWN], F32)
    cos2 = p.inp("cos2", [128, TOWN], F32)
    sin2 = p.inp("sin2", [128, TOWN], F32)
    cPd = p.inp("Pdeint", [64, 64], BF16)
    cPr = p.inp("Prot", [64, 64], BF16)
    cPd2 = p.inp("Pdeint2", [128, 128], BF16)
    cPr2 = p.inp("Prot2", [128, 128], BF16)
    c_ones_col = p.inp("ones_col", [128, 1], BF16)
    c_ones_row = p.inp("ones_row", [1, 128], BF16)
    c_I = p.inp("I128", [128, 128], BF16)

    o_xn1 = p.out("xn1T", [D, TOWN], BF16)
    o_qn = p.out("qnopeT", [2048, TOWN], BF16)
    o_qp = p.out("qpeT", [64, 16 * TOWN], BF16)
    o_ckv = p.out("ckvT", [512, TOWN], BF16)
    o_kpe = p.out("kpeT", [64, TOWN], BF16)
    o_kn = p.out("knopeT", [2048, TOWN], BF16)
    o_v = p.out("v1", [TOWN, 2048], BF16)
    o_xlat = p.out("xlatT", [1024, TOWN], BF16)
    o_cwsq = p.out("cwsqT", [32, TOWN], F32)
    o_qn2 = p.out("qn2", [1, 16 * TOWN], F32)
    o_kn2 = p.out("kn2", [1, 16], F32)
    o_sco = p.out("sco_dbg", [128, 32], F32)

    xn_all = p.sb("xn1", [128, 16 * TOWN], BF16)
    xn = [xn_all[:, i * TOWN:(i + 1) * TOWN] for i in range(16)]
    onec = p.sb("onec", [128, 1], BF16)
    onesr = p.sb("onesr", [1, 128], BF16)
    onecf = p.sb("onecf", [128, 1], F32)
    isb = p.sb("isb", [128, 128], BF16)
    row = p.sb("row", [1, 4 * TOWN], F32)
    rowb = p.sb("rowb", [1, 2 * TOWN], BF16)
    f32tmp = p.sb("f32t", [128, TOWN], F32)
    f32u = p.sb("f32u", [128, TOWN], F32)
    sqn = p.sb("sqn", [128, TOWN], F32)
    obf = p.sb("obf", [128, 2 * TOWN], BF16)
    rowq = p.sb("rowq", [1, TOWN], F32)
    kn2sb = p.sb("kn2sb", [1, 16], F32)
    psA = p.ps("psA", [128, TOWN])
    psB = p.ps("psB", [128, TOWN])
    psC = p.ps("psC", [128, TOWN])
    psR = p.ps("psR", [1, TOWN])
    dsem = p.sem("dsem")
    vself = p.sem("vself")
    nv = [0]

    def dma(eng, out, in_):
        eng.dma_start(out=out, in_=in_).then_inc(dsem, 16)
        p.dcnt += 16

    def gload(items):
        def gfn(g):
            for out, in_ in items:
                dma(g, out, in_)
            g.wait_ge(dsem, p.dcnt)
        B(nc, gpsimd=gfn)

    def sload(items):
        def syfn(sy):
            for out, in_ in items:
                dma(sy, out, in_)
            sy.wait_ge(dsem, p.dcnt)
        B(nc, sync=syfn)

    sload([(onec[:, :], c_ones_col), (onesr[:, :], c_ones_row),
           (isb[:, :], c_I)])
    onesrf = p.sb("onesrf", [1, 128], F32)
    B(nc, vector=lambda v: [v.memset(onecf[:, :], 1.0),
                            v.memset(onesrf[:, :], 1.0)])

    ln_ctx = ExitStack()
    xb_all = p.sb("xb1", [128, 16 * TOWN], BF16, ln_ctx)
    xb = [xb_all[:, i * TOWN:(i + 1) * TOWN] for i in range(16)]
    gload([(xb[i], x1T[128 * i:128 * (i + 1), :]) for i in range(16)])
    for i in range(16):
        B(nc, scalar=lambda s, i=i: s.activation(sqn[:, :], xb[i], AF.Square))
        B(nc, tensor=lambda t, i=i: [
            [t.matmul(psR[:, c0:c0 + 512], onec[:, :], xb[i][:, c0:c0 + 512],
                      start=(i == 0), stop=(i == 15)),
             t.matmul(psA[0:1, c0:c0 + 512], onecf[:, :], sqn[:, c0:c0 + 512],
                      start=(i == 0), stop=(i == 15))] for c0 in (0, 512)])

    def _lnfin(v):
        mu = row[0:1, 0:TOWN]
        e2 = row[0:1, TOWN:2 * TOWN]
        var = row[0:1, 2 * TOWN:3 * TOWN]
        v.tensor_scalar(mu, psR[0:1, :], 1.0 / D, None, ALU.mult)
        v.tensor_scalar(e2, psA[0:1, :], 1.0 / D, None, ALU.mult)
        v.tensor_tensor(var, mu, mu, ALU.mult)
        v.tensor_tensor(var, e2, var, ALU.subtract)
        v.tensor_single_scalar(var, var, 1e-5, ALU.add)
    B(nc, vector=_lnfin)
    B(nc, scalar=lambda s: s.activation(row[0:1, 3 * TOWN:4 * TOWN],
                                        row[0:1, 2 * TOWN:3 * TOWN], AF.Ln))
    B(nc, scalar=lambda s: s.activation(row[0:1, 3 * TOWN:4 * TOWN],
                                        row[0:1, 3 * TOWN:4 * TOWN], AF.Exp,
                                        scale=-0.5))

    def _lnab(v):
        mu = row[0:1, 0:TOWN]
        a = row[0:1, 3 * TOWN:4 * TOWN]
        v.tensor_copy(rowb[0:1, 0:TOWN], a)
        v.tensor_tensor(row[0:1, TOWN:2 * TOWN], mu, a, ALU.mult)
        v.tensor_scalar(rowb[0:1, TOWN:2 * TOWN], row[0:1, TOWN:2 * TOWN],
                        -1.0, None, ALU.mult)
    B(nc, vector=_lnab)
    B(nc, tensor=lambda t: [
        mm_acc(t, psA[:, :], [(onesr[:, :], rowb[0:1, 0:TOWN])], TOWN),
        mm_acc(t, psB[:, :], [(onesr[:, :], rowb[0:1, TOWN:2 * TOWN])], TOWN)])

    def _lnapply(v):
        for i in range(16):
            v.tensor_tensor(f32tmp[:, :], xb[i], psA[:, :], ALU.mult)
            v.tensor_tensor(xn[i], f32tmp[:, :], psB[:, :], ALU.add)
    B(nc, vector=_lnapply)
    B(nc, sync=lambda sy: [[dma(sy, o_xn1[128 * i:128 * (i + 1), :], xn[i])
                            for i in range(16)], sy.wait_ge(dsem, p.dcnt)])
    ln_ctx.close()

    q_ctx = ExitStack()
    qa_all = p.sb("qa", [128, 12 * TOWN], BF16, q_ctx)
    qa = [qa_all[:, i * TOWN:(i + 1) * TOWN] for i in range(12)]
    wq_ctx = ExitStack()
    wq3 = p.sb("wq3", [128, 16, 1536], BF16, wq_ctx)
    gload([(wq3[:, i, :], wqa[128 * i:128 * (i + 1), :]) for i in range(16)])
    for oc in range(12):
        B(nc, tensor=lambda t, oc=oc: mm_acc(
            t, psA[:, :], [(wq3[:, i, 128 * oc:128 * (oc + 1)], xn[i])
                           for i in range(16)], TOWN))
        B(nc, vector=lambda v, oc=oc: v.tensor_copy(qa[oc], psA[:, :]),
          scalar=lambda s: s.activation(sqn[:, :], psA[:, :], AF.Square))
        B(nc, tensor=lambda t, oc=oc: [
            t.matmul(psR[:, c0:c0 + 512], onecf[:, :], sqn[:, c0:c0 + 512],
                     start=(oc == 0), stop=(oc == 11)) for c0 in (0, 512)])

    def _rmsq(v):
        var = row[0:1, 2 * TOWN:3 * TOWN]
        v.tensor_scalar(var, psR[0:1, :], 1.0 / 1536.0, None, ALU.mult)
        v.tensor_single_scalar(var, var, 1e-6, ALU.add)
    B(nc, vector=_rmsq)
    wq_ctx.close()
    B(nc, scalar=lambda s: s.activation(row[0:1, 3 * TOWN:4 * TOWN],
                                        row[0:1, 2 * TOWN:3 * TOWN], AF.Ln))
    B(nc, scalar=lambda s: s.activation(row[0:1, 3 * TOWN:4 * TOWN],
                                        row[0:1, 3 * TOWN:4 * TOWN], AF.Exp,
                                        scale=-0.5))
    B(nc, vector=lambda v: v.tensor_copy(rowb[0:1, 0:TOWN],
                                         row[0:1, 3 * TOWN:4 * TOWN]))
    B(nc, tensor=lambda t: mm_acc(t, psC[:, :],
                                  [(onesr[:, :], rowb[0:1, 0:TOWN])], TOWN))
    lnq = p.sb("lnq", [128, 12], F32, q_ctx)
    sload([(lnq[:, :], qaln)])

    def _qaapply(v):
        for oc in range(12):
            v.tensor_tensor(f32tmp[:, :], qa[oc], psC[:, :], ALU.mult)
            v.tensor_scalar(qa[oc], f32tmp[:, :], lnq[:, oc:oc + 1], None,
                            ALU.mult)
    B(nc, vector=_qaapply)
    wb2 = p.sb("wb2", [128, 12, 1024], BF16, q_ctx)
    for half in range(2):
        gload([(wb2[:, i, :], wqbn[128 * i:128 * (i + 1),
                                   1024 * half:1024 * (half + 1)])
               for i in range(12)])
        for oc in range(8):
            h = 8 * half + oc
            B(nc, tensor=lambda t, oc=oc: mm_acc(
                t, psA[:, :], [(wb2[:, i, 128 * oc:128 * (oc + 1)], qa[i])
                               for i in range(12)], TOWN))
            B(nc, vector=lambda v: v.tensor_copy(obf[:, 0:TOWN], psA[:, :]),
              scalar=lambda s: s.activation(sqn[:, :], psA[:, :], AF.Square))
            B(nc, tensor=lambda t: mm_acc(t, psR[:, :],
                                          [(onecf[:, :], sqn[:, :])], TOWN))
            B(nc, sync=lambda sy, h=h: [dma(sy, o_qn[128 * h:128 * (h + 1), :],
                                            obf[:, 0:TOWN]),
                                        sy.wait_ge(dsem, p.dcnt)],
              vector=lambda v: v.tensor_copy(rowq[0:1, :], psR[0:1, :]))
            B(nc, sync=lambda sy, h=h: [
                dma(sy, o_qn2[0:1, TOWN * h:TOWN * (h + 1)], rowq[0:1, :]),
                sy.wait_ge(dsem, p.dcnt)])
    wb3 = p.sb("wb3", [128, 12, 1024], BF16, q_ctx)
    p2d = p.sb("p2d", [128, 128], BF16, q_ctx)
    p2r = p.sb("p2r", [128, 128], BF16, q_ctx)
    cos2b = p.sb("cos2b", [128, TOWN], F32, q_ctx)
    sin2b = p.sb("sin2b", [128, TOWN], F32, q_ctx)
    gload([(wb3[:, i, :], wqbp[128 * i:128 * (i + 1), :]) for i in range(12)])
    sload([(p2d[:, :], cPd2), (p2r[:, :], cPr2),
           (cos2b[:, :], cos2), (sin2b[:, :], sin2)])
    for pc in range(8):
        B(nc, tensor=lambda t, pc=pc: mm_acc(
            t, psA[:, :], [(wb3[:, i, 128 * pc:128 * (pc + 1)], qa[i])
                           for i in range(12)], TOWN))
        B(nc, vector=lambda v: v.tensor_copy(obf[:, 0:TOWN], psA[:, :]))
        B(nc, tensor=lambda t: [
            mm_acc(t, psA[:, :], [(p2d[:, :], obf[:, 0:TOWN])], TOWN),
            mm_acc(t, psB[:, :], [(p2r[:, :], obf[:, 0:TOWN])], TOWN)])

        def _rope(v):
            v.tensor_tensor(f32tmp[:, :], psA[:, :], cos2b[:, :], ALU.mult)
            v.tensor_tensor(f32u[:, :], psB[:, :], sin2b[:, :], ALU.mult)
            v.tensor_tensor(f32tmp[:, :], f32tmp[:, :], f32u[:, :], ALU.add)
        B(nc, vector=_rope)
        B(nc, vector=lambda v: [v.tensor_copy(obf[0:64, 0:TOWN], f32tmp[0:64, :]),
                                v.tensor_copy(obf[0:64, TOWN:2 * TOWN],
                                              f32tmp[64:128, :])],
          scalar=lambda s: s.activation(sqn[:, :], f32tmp[:, :], AF.Square))
        B(nc, sync=lambda sy, pc=pc: [
            dma(sy, o_qp[:, TOWN * 2 * pc:TOWN * (2 * pc + 1)],
                obf[0:64, 0:TOWN]),
            dma(sy, o_qp[:, TOWN * (2 * pc + 1):TOWN * (2 * pc + 2)],
                obf[0:64, TOWN:2 * TOWN]),
            sy.wait_ge(dsem, p.dcnt)],
          vector=lambda v: v.tensor_copy(f32u[0:64, :], sqn[64:128, :]))
        for which, srctile in ((0, sqn), (1, f32u)):
            h2 = 2 * pc + which
            B(nc, tensor=lambda t, srctile=srctile: mm_acc(
                t, psR[:, :], [(onecf[0:64, :], srctile[0:64, :])], TOWN))
            B(nc, sync=lambda sy, h2=h2: [
                dma(sy, rowq[0:1, :], o_qn2[0:1, TOWN * h2:TOWN * (h2 + 1)]),
                sy.wait_ge(dsem, p.dcnt)])
            B(nc, vector=lambda v: v.tensor_tensor(rowq[0:1, :], rowq[0:1, :],
                                                   psR[0:1, :], ALU.add))
            B(nc, sync=lambda sy, h2=h2: [
                dma(sy, o_qn2[0:1, TOWN * h2:TOWN * (h2 + 1)], rowq[0:1, :]),
                sy.wait_ge(dsem, p.dcnt)])
    q_ctx.close()

    k_ctx = ExitStack()
    wk3 = p.sb("wk3", [128, 16, 576], BF16, k_ctx)
    ckvb_all = p.sb("ckvb", [128, 4 * TOWN], BF16, k_ctx)
    ckvb = [ckvb_all[:, i * TOWN:(i + 1) * TOWN] for i in range(4)]
    ckvf_all = p.sb("ckvf", [128, 4 * TOWN], F32, k_ctx)
    ckvf = [ckvf_all[:, i * TOWN:(i + 1) * TOWN] for i in range(4)]
    lnk = p.sb("lnk", [128, 4], F32, k_ctx)
    pdd = p.sb("pdd", [64, 64], BF16, k_ctx)
    pdr = p.sb("pdr", [64, 64], BF16, k_ctx)
    cos1b = p.sb("cos1b", [64, TOWN], F32, k_ctx)
    sin1b = p.sb("sin1b", [64, TOWN], F32, k_ctx)
    gload([(wk3[:, i, :], wkva[128 * i:128 * (i + 1), :]) for i in range(16)])
    sload([(lnk[:, :], kvln),
           (pdd[:, :], cPd), (pdr[:, :], cPr),
           (cos1b[:, :], cos1), (sin1b[:, :], sin1)])
    for oc in range(4):
        B(nc, tensor=lambda t, oc=oc: mm_acc(
            t, psA[:, :], [(wk3[:, i, 128 * oc:128 * (oc + 1)], xn[i])
                           for i in range(16)], TOWN))
        B(nc, vector=lambda v, oc=oc: v.tensor_copy(ckvf[oc], psA[:, :]),
          scalar=lambda s: s.activation(sqn[:, :], psA[:, :], AF.Square))
        B(nc, tensor=lambda t, oc=oc: [
            t.matmul(psR[:, c0:c0 + 512], onecf[:, :], sqn[:, c0:c0 + 512],
                     start=(oc == 0), stop=(oc == 3)) for c0 in (0, 512)])

    def _ckvrms(v):
        var = row[0:1, 2 * TOWN:3 * TOWN]
        v.tensor_scalar(var, psR[0:1, :], 1.0 / 512.0, None, ALU.mult)
        v.tensor_single_scalar(var, var, 1e-6, ALU.add)
    B(nc, vector=_ckvrms)
    B(nc, scalar=lambda s: s.activation(row[0:1, 3 * TOWN:4 * TOWN],
                                        row[0:1, 2 * TOWN:3 * TOWN], AF.Ln))
    B(nc, scalar=lambda s: s.activation(row[0:1, 3 * TOWN:4 * TOWN],
                                        row[0:1, 3 * TOWN:4 * TOWN], AF.Exp,
                                        scale=-0.5))
    B(nc, vector=lambda v: v.tensor_copy(rowb[0:1, 0:TOWN],
                                         row[0:1, 3 * TOWN:4 * TOWN]))
    B(nc, tensor=lambda t: mm_acc(t, psC[:, :],
                                  [(onesr[:, :], rowb[0:1, 0:TOWN])], TOWN))

    def _ckvapply(v):
        for oc in range(4):
            v.tensor_tensor(f32tmp[:, :], ckvf[oc], psC[:, :], ALU.mult)
            v.tensor_scalar(ckvb[oc], f32tmp[:, :], lnk[:, oc:oc + 1], None,
                            ALU.mult)
    B(nc, vector=_ckvapply)
    B(nc, sync=lambda sy: [[dma(sy, o_ckv[128 * i:128 * (i + 1), :], ckvb[i])
                            for i in range(4)], sy.wait_ge(dsem, p.dcnt)])
    B(nc, tensor=lambda t: mm_acc(t, psB[0:64, :],
                                  [(wk3[:, i, 512:576], xn[i])
                                   for i in range(16)], TOWN))
    B(nc, vector=lambda v: v.tensor_copy(obf[0:64, 0:TOWN], psB[0:64, :]))
    B(nc, tensor=lambda t: [
        mm_acc(t, psB[0:64, :], [(pdd[:, :], obf[0:64, 0:TOWN])], TOWN),
        mm_acc(t, psC[0:64, :], [(pdr[:, :], obf[0:64, 0:TOWN])], TOWN)])

    def _krope(v):
        v.tensor_tensor(f32tmp[0:64, :], psB[0:64, :], cos1b[:, :], ALU.mult)
        v.tensor_tensor(f32u[0:64, :], psC[0:64, :], sin1b[:, :], ALU.mult)
        v.tensor_tensor(obf[0:64, 0:TOWN], f32tmp[0:64, :], f32u[0:64, :],
                        ALU.add)
    B(nc, vector=_krope)
    B(nc, sync=lambda sy: [dma(sy, o_kpe, obf[0:64, 0:TOWN]),
                           sy.wait_ge(dsem, p.dcnt)])
    B(nc, scalar=lambda s: s.activation(sqn[0:64, :], obf[0:64, 0:TOWN],
                                        AF.Square))
    B(nc, tensor=lambda t: mm_acc(t, psR[:, :],
                                  [(onecf[0:64, :], sqn[0:64, :])], TOWN))
    B(nc, vector=lambda v: v.tensor_copy(row[0:1, TOWN:2 * TOWN], psR[0:1, :]))

    wn3 = p.sb("wn3", [128, 4, 2048], BF16, k_ctx)
    gload([(wn3[:, i, :], wkvbn[128 * i:128 * (i + 1), :]) for i in range(4)])
    for h in range(16):
        B(nc, tensor=lambda t, h=h: mm_acc(
            t, psA[:, :], [(wn3[:, i, 128 * h:128 * (h + 1)], ckvb[i])
                           for i in range(4)], TOWN))
        B(nc, vector=lambda v: v.tensor_copy(obf[:, 0:TOWN], psA[:, :]),
          scalar=lambda s: s.activation(sqn[:, :], psA[:, :], AF.Square))
        B(nc, tensor=lambda t: mm_acc(t, psR[:, :],
                                      [(onecf[:, :], sqn[:, :])], TOWN))
        B(nc, sync=lambda sy, h=h: [dma(sy, o_kn[128 * h:128 * (h + 1), :],
                                        obf[:, 0:TOWN]),
                                    sy.wait_ge(dsem, p.dcnt)],
          vector=lambda v: v.tensor_tensor(f32tmp[0:1, :], psR[0:1, :],
                                           row[0:1, TOWN:2 * TOWN], ALU.add))
        B(nc, vector=lambda v, h=h: v.reduce_max(kn2sb[0:1, h:h + 1],
                                                 f32tmp[0:1, :], axis=AX.X))
    B(nc, sync=lambda sy: [dma(sy, o_kn2, kn2sb[:, :]),
                           sy.wait_ge(dsem, p.dcnt)])
    wv3 = p.sb("wv3", [128, 4, 2048], BF16, k_ctx)
    gload([(wv3[:, i, :], wkvbv[128 * i:128 * (i + 1), :]) for i in range(4)])
    for tt in range(8):
        B(nc, tensor=lambda t, tt=tt: [
            mm_acc(t, psA[:, :],
                   [(ckvb[i][:, 128 * tt:128 * (tt + 1)], wv3[:, i, 0:1024])
                    for i in range(4)], TOWN),
            mm_acc(t, psB[:, :],
                   [(ckvb[i][:, 128 * tt:128 * (tt + 1)], wv3[:, i, 1024:2048])
                    for i in range(4)], TOWN)])
        B(nc, vector=lambda v: [v.tensor_copy(obf[:, 0:TOWN], psA[:, :]),
                                v.tensor_copy(obf[:, TOWN:2 * TOWN], psB[:, :])])
        B(nc, sync=lambda sy, tt=tt: [
            dma(sy, o_v[128 * tt:128 * (tt + 1), 0:1024], obf[:, 0:TOWN]),
            dma(sy, o_v[128 * tt:128 * (tt + 1), 1024:2048],
                obf[:, TOWN:2 * TOWN]),
            sy.wait_ge(dsem, p.dcnt)])
    k_ctx.close()

    r_ctx = ExitStack()
    gw3 = p.sb("gw3", [128, 16, 32], BF16, r_ctx)
    gb = p.sb("gb", [128, 32], F32, r_ctx)
    gbrow = p.sb("gbrow", [1, 32], F32, r_ctx)
    sco = p.sb("sco", [128, 32], F32, r_ctx)
    sb2 = p.sb("sb2", [128, 32], F32, r_ctx)
    m0 = p.sb("m0", [128, 32], F32, r_ctx)
    wk_ = p.sb("wk_", [128, 32], F32, r_ctx)
    m1 = p.sb("m1", [128, 4], F32, r_ctx)
    m2 = p.sb("m2", [128, 4], F32, r_ctx)
    gsr = p.sb("gsr", [128, 4], F32, r_ctx)
    gm = p.sb("gm", [128, 2], F32, r_ctx)
    gmk = p.sb("gmk", [128, 4], F32, r_ctx)
    e8t = p.sb("e8t", [128, 8], F32, r_ctx)
    mx1 = p.sb("mx1", [128, 1], F32, r_ctx)
    selw = p.sb("selw", [128, 32], F32, r_ctx)
    cwbf = p.sb("cwbf", [128, 32], BF16, r_ctx)
    cwsqT = p.sb("cwsqT_sb", [32, TOWN], F32, r_ctx)
    gload([(gw3[:, i, :], gwT[128 * i:128 * (i + 1), :]) for i in range(16)])
    sload([(gbrow[0:1, :], gbias)])
    B(nc, tensor=lambda t: t.matmul(psB[:, 0:32], onesrf[:, :], gbrow[0:1, :],
                                    start=True, stop=True))
    B(nc, vector=lambda v: v.tensor_copy(gb[:, :], psB[:, 0:32]))

    def vcv(v, instr):
        nv[0] += 1
        instr.then_inc(vself, 1)
        v.wait_ge(vself, nv[0])

    for tt in range(8):
        B(nc, tensor=lambda t, tt=tt: mm_acc(
            t, psA[:, 0:32], [(xn[i][:, 128 * tt:128 * (tt + 1)], gw3[:, i, :])
                              for i in range(16)], 32))
        B(nc, scalar=lambda s: s.activation(sco[:, :], psA[:, 0:32], AF.Sigmoid))
        if tt == 0:
            B(nc, sync=lambda sy: [dma(sy, o_sco, sco[:, :]),
                                   sy.wait_ge(dsem, p.dcnt)])

        def _route1(v):
            vcv(v, v.tensor_tensor(m0[:, :], sco[:, :], gb[:, :], ALU.add))
            for g in range(4):
                vcv(v, v.reduce_max(m1[:, g:g + 1], m0[:, 8 * g:8 * (g + 1)],
                                    axis=AX.X))
            for g in range(4):
                vcv(v, v.tensor_scalar(e8t[:, 0:8], m0[:, 8 * g:8 * (g + 1)],
                                       m1[:, g:g + 1], None, ALU.is_ge))
                vcv(v, v.scalar_tensor_tensor(sb2[:, 8 * g:8 * (g + 1)],
                                              e8t[:, 0:8], -BIGR,
                                              m0[:, 8 * g:8 * (g + 1)],
                                              ALU.mult, ALU.add))
                vcv(v, v.reduce_max(m2[:, g:g + 1], sb2[:, 8 * g:8 * (g + 1)],
                                    axis=AX.X))
            vcv(v, v.tensor_tensor(gsr[:, :], m1[:, :], m2[:, :], ALU.add))
            vcv(v, v.reduce_max(gm[:, 0:1], gsr[:, :], axis=AX.X))
            vcv(v, v.tensor_scalar(gmk[:, :], gsr[:, :], gm[:, 0:1], None,
                                   ALU.is_ge))
            vcv(v, v.scalar_tensor_tensor(m2[:, :], gmk[:, :], -BIGR, gsr[:, :],
                                          ALU.mult, ALU.add))
            vcv(v, v.reduce_max(gm[:, 1:2], m2[:, :], axis=AX.X))
            vcv(v, v.tensor_scalar(gmk[:, :], gsr[:, :], gm[:, 1:2], None,
                                   ALU.is_ge))
            vcv(v, v.tensor_scalar(gmk[:, :], gmk[:, :], -BIGR, BIGR,
                                   ALU.mult, ALU.add))
            vcv(v, v.tensor_scalar(gmk[:, :], gmk[:, :], -1.0, None, ALU.mult))
            for g in range(4):
                vcv(v, v.tensor_scalar(m0[:, 8 * g:8 * (g + 1)],
                                       m0[:, 8 * g:8 * (g + 1)],
                                       gmk[:, g:g + 1], None, ALU.add))
            vcv(v, v.tensor_copy(wk_[:, :], m0[:, :]))
            for it in range(7):
                vcv(v, v.reduce_max(mx1[:, 0:1], wk_[:, :], axis=AX.X))
                vcv(v, v.tensor_scalar(selw[:, :], wk_[:, :], mx1[:, 0:1], None,
                                       ALU.is_ge))
                vcv(v, v.scalar_tensor_tensor(wk_[:, :], selw[:, :], -BIGR,
                                              wk_[:, :], ALU.mult, ALU.add))
            vcv(v, v.reduce_max(mx1[:, 0:1], wk_[:, :], axis=AX.X))
            vcv(v, v.tensor_scalar(selw[:, :], m0[:, :], mx1[:, 0:1], None,
                                   ALU.is_ge))
            vcv(v, v.tensor_tensor(selw[:, :], sco[:, :], selw[:, :], ALU.mult))
            vcv(v, v.reduce_sum(mx1[:, 0:1], selw[:, :], axis=AX.X))
            vcv(v, v.reciprocal(mx1[:, 0:1], mx1[:, 0:1]))
            vcv(v, v.tensor_scalar(selw[:, :], selw[:, :], mx1[:, 0:1], 2.5,
                                   ALU.mult, ALU.mult))
        B(nc, vector=_route1)
        B(nc, scalar=lambda s: s.activation(cwbf[:, :], selw[:, :], AF.Sqrt))
        B(nc, tensor=lambda t: t.matmul(psA[0:32, 0:128], cwbf[:, :], isb[:, :],
                                        start=True, stop=True))
        B(nc, vector=lambda v, tt=tt: v.tensor_copy(
            cwsqT[:, 128 * tt:128 * (tt + 1)], psA[0:32, 0:128]))
    B(nc, sync=lambda sy: [dma(sy, o_cwsq, cwsqT[:, :]),
                           sy.wait_ge(dsem, p.dcnt)])
    r_ctx.close()

    xl_ctx = ExitStack()
    wf3 = p.sb("wf3", [128, 16, 1024], BF16, xl_ctx)
    gload([(wf3[:, i, :], fc1T[128 * i:128 * (i + 1), :]) for i in range(16)])
    for oc in range(8):
        B(nc, tensor=lambda t, oc=oc: mm_acc(
            t, psA[:, :], [(wf3[:, i, 128 * oc:128 * (oc + 1)], xn[i])
                           for i in range(16)], TOWN))
        B(nc, vector=lambda v: v.tensor_copy(obf[:, 0:TOWN], psA[:, :]))
        B(nc, sync=lambda sy, oc=oc: [dma(sy, o_xlat[128 * oc:128 * (oc + 1), :],
                                          obf[:, 0:TOWN]),
                                      sy.wait_ge(dsem, p.dcnt)])
    xl_ctx.close()
    p.ctx.close()
    return nc


def pre1_inputs(full, x1_by_core):
    wqb = np.asarray(full["L1_attn_w_q_b"])
    cols = np.arange(3072).reshape(16, 192)
    wqbn = np.ascontiguousarray(wqb[:, cols[:, :128].reshape(-1)])
    wqbp = np.ascontiguousarray(wqb[:, cols[:, 128:].reshape(-1)])
    wkvb = np.asarray(full["L1_attn_w_kv_b"])
    cols2 = np.arange(4096).reshape(16, 256)
    wkvbn = np.ascontiguousarray(wkvb[:, cols2[:, :128].reshape(-1)])
    wkvbv = np.ascontiguousarray(wkvb[:, cols2[:, 128:].reshape(-1)])
    maps = []
    for c in range(NCORES):
        idx = own_token_index(c)
        maps.append({
            "x1T": np.asarray(x1_by_core[c]),
            "wqa": f32c(full["L1_attn_w_q_a"]),
            "wqbn": f32c(wqbn), "wqbp": f32c(wqbp),
            "qaln": f32c(np.asarray(full["L1_attn_q_a_ln"]).reshape(12, 128).T),
            "wkva": f32c(full["L1_attn_w_kv_a"]),
            "kvln": f32c(np.asarray(full["L1_attn_kv_a_ln"]).reshape(4, 128).T),
            "wkvbn": f32c(wkvbn), "wkvbv": f32c(wkvbv),
            "gwT": f32c(np.asarray(full["L1_ffn_gate_w"]).T),
            "gbias": f32c(np.asarray(full["L1_ffn_gate_bias"]).reshape(1, 32)),
            "fc1T": f32c(np.asarray(full["L1_ffn_fc1_w"]).T),
            "cos1": f32c(CONSTS["cos1"][:, idx]),
            "sin1": f32c(CONSTS["sin1"][:, idx]),
            "cos2": f32c(np.tile(CONSTS["cos1"][:, idx], (2, 1))),
            "sin2": f32c(np.tile(CONSTS["sin1"][:, idx], (2, 1))),
            "Pdeint": CONSTS["Pdeint"], "Prot": CONSTS["Prot"],
            "Pdeint2": bfc(np.block([[np.asarray(CONSTS["Pdeint"], np.float32), np.zeros((64, 64))],
                                     [np.zeros((64, 64)), np.asarray(CONSTS["Pdeint"], np.float32)]])),
            "Prot2": bfc(np.block([[np.asarray(CONSTS["Prot"], np.float32), np.zeros((64, 64))],
                                   [np.zeros((64, 64)), np.asarray(CONSTS["Prot"], np.float32)]])),
            "ones_col": CONSTS["ones_col"], "ones_row": CONSTS["ones_row"],
            "I128": CONSTS["I128"],
        })
    return maps


def run_pre1(full, x1_by_core):
    nc = build_pre1()
    res = run_bass_kernel_spmd(nc, pre1_inputs(full, x1_by_core),
                               core_ids=list(range(NCORES)))
    return res.results



def build_flash1():
    p = P()
    nc = p.nc
    qn = p.inp("qn", [2048, TOWN], BF16)
    qp = p.inp("qp", [64, 16 * TOWN], BF16)
    qn2 = p.inp("qn2", [1, 16 * TOWN], F32)
    kn2 = p.inp("kn2", [1, 128], F32)
    kn_job = p.inp("kn_job", [16, 3, 128, 4096], BF16)
    v_job = p.inp("v_job", [16, 3, 128, 4096], BF16)
    kpe_job = p.inp("kpe_job", [64, 24 * 512], BF16)
    msk_job = p.inp("msk_job", [128, 24 * 2048], BF16)
    Wout = p.inp("Wout", [D, D], F32)
    c_I = p.inp("I128", [128, 128], BF16)
    c_onec = p.inp("ones_col", [128, 1], BF16)

    o_attn = p.out("attn1T", [D, TOWN], F32)

    onorm_dram = p.dram("onorm1_dram", [D, TOWN], BF16)
    oacc_dram = p.dram("oacc1_dram", [D, 512], F32)
    sums_dram = p.dram("sums1_dram", [16, 512], F32)
    crow_dram = p.dram("crow1_dram", [1, 16 * TOWN], F32)

    isb = p.sb("isb", [128, 128], BF16)
    onecb = p.sb("onecb", [128, 1], BF16)
    onesr_f = p.sb("onesr_f", [1, 128], F32)
    kn2sb = p.sb("kn2sb", [1, 128], F32)
    mrow = p.sb("mrow", [1, 16], F32)
    dsem = p.sem("dsem")
    tsem = p.sem("tsem")
    esem = p.sem("esem")
    vsem = p.sem("vsem")
    osem = p.sem("osem")
    ksem = p.sem("ksem")
    stsem = p.sem("stsem")
    wsem = p.sem("wsem")
    tsem2 = p.sem("tsem2")
    vsem2 = p.sem("vsem2")

    def dma(eng, out, in_):
        eng.dma_start(out=out, in_=in_).then_inc(dsem, 16)
        p.dcnt += 16

    pp_ctx = ExitStack()
    qn2sb = p.sb("qn2sb", [1, 16 * TOWN], F32, pp_ctx)
    crow = p.sb("crow", [1, 16 * TOWN], F32, pp_ctx)
    with nc.Block() as blk:
        @blk.sync
        def _(sy):
            dma(sy, isb[:, :], c_I)
            dma(sy, onecb[:, :], c_onec)
            dma(sy, qn2sb[:, :], qn2)
            dma(sy, kn2sb[:, :], kn2)
            sy.wait_ge(dsem, p.dcnt)

        @blk.vector
        def _(v):
            v.memset(onesr_f[:, :], 1.0)
    with nc.Block() as blk:
        @blk.vector
        def _(v):
            for h in range(16):
                v.reduce_max(mrow[0:1, h:h + 1], kn2sb[0:1, 8 * h:8 * (h + 1)],
                             axis=AX.X)
    with nc.Block() as blk:
        @blk.vector
        def _(v):
            for h in range(16):
                v.tensor_scalar(crow[0:1, TOWN * h:TOWN * (h + 1)],
                                qn2sb[0:1, TOWN * h:TOWN * (h + 1)],
                                mrow[0:1, h:h + 1], None, ALU.mult)
    with nc.Block() as blk:
        @blk.scalar
        def _(s):
            for h in range(16):
                s.activation(crow[0:1, TOWN * h:TOWN * (h + 1)],
                             crow[0:1, TOWN * h:TOWN * (h + 1)], AF.Sqrt)
    with nc.Block() as blk:
        @blk.vector
        def _(v):
            for h in range(16):
                sl = crow[0:1, TOWN * h:TOWN * (h + 1)]
                v.tensor_scalar(sl, sl, -1.0, -5.0 * SQD1, ALU.mult, ALU.add)
    with nc.Block() as blk:
        @blk.sync
        def _(sy):
            dma(sy, crow_dram, crow[0:1, :])
            sy.wait_ge(dsem, p.dcnt)
    pp_ctx.close()

    fa_ctx = ExitStack()
    crow_ph = p.sb("crow_ph", [1, 16 * 512], F32, fa_ctx)
    qn_ph = p.sb("qn_ph", [128, 16 * 512], BF16, fa_ctx)
    qp_ph = p.sb("qp_ph", [64, 16 * 512], BF16, fa_ctx)
    kpesb = p.sb("kpesb", [64, 24 * 512], BF16, fa_ctx)
    mskph = p.sb("mskph", [128, 32 * 512], BF16, fa_ctx)
    ksb = [p.sb(f"ksb{i}", [128, 4096], BF16, fa_ctx) for i in range(2)]
    vsb = [p.sb(f"vsb{i}", [128, 4096], BF16, fa_ctx) for i in range(2)]
    Pt_all = p.sb("Pt", [128, 2 * 512], BF16, fa_ctx)
    ob32 = p.sb("ob32", [128, 512], F32, fa_ctx)
    srow = p.sb("srow", [1, 512], F32, fa_ctx)
    sps = [p.ps(f"sps{i}", [128, 512], ctx=fa_ctx) for i in range(2)]
    outp = p.ps("outp", [128, 512], ctx=fa_ctx)
    sump = p.ps("sump", [1, 512], ctx=fa_ctx)
    Pt = [Pt_all[:, 0:512], Pt_all[:, 512:1024]]

    B(nc, sync=lambda sy: [dma(sy, kpesb[:, :], kpe_job),
                           sy.wait_ge(dsem, p.dcnt)])

    ctr = dict(t=0, e=0, v=0, o=0, k=0)

    def emit_phase(ph):
        qc0 = 0 if ph == 0 else 512
        t0, e0, v0, o0, k0 = (ctr["t"], ctr["e"], ctr["v"], ctr["o"], ctr["k"])
        accum = (ph == 2)
        with nc.Block() as blk:
            @blk.sync
            def _(sy):
                dma(sy, mskph[:, :],
                    msk_job[:, (ph * 8) * 2048:(ph * 8 + 8) * 2048])
                for i in range(16):
                    dma(sy, qn_ph[:, 512 * i:512 * (i + 1)],
                        qn[128 * i:128 * (i + 1), qc0:qc0 + 512])
                    dma(sy, qp_ph[:, 512 * i:512 * (i + 1)],
                        qp[:, TOWN * i + qc0:TOWN * i + qc0 + 512])
                for h in range(16):
                    dma(sy, crow_ph[0:1, 512 * h:512 * (h + 1)],
                        crow_dram[0:1, TOWN * h + qc0:TOWN * h + qc0 + 512])
                for h in range(16):
                    if h >= 2:
                        sy.wait_ge(tsem, t0 + 33 * (h - 1))
                    sy.dma_start(out=ksb[h % 2][:, :],
                                 in_=kn_job[h, ph]).then_inc(ksem, 16)
                    sy.dma_start(out=vsb[h % 2][:, :],
                                 in_=v_job[h, ph]).then_inc(ksem, 16)

            @blk.tensor
            def _(t):
                t.wait_ge(dsem, p.dcnt)
                for h in range(16):
                    t.wait_ge(ksem, k0 + 32 * (h + 1))
                    for ch in range(33):
                        if ch >= 1:
                            t.wait_ge(esem, e0 + 32 * h + ch)
                            if ch == 1:
                                t.wait_ge(vsem, v0 + h)
                            pc = ch - 1
                            t.matmul(outp[:, :],
                                     vsb[h % 2][:, 128 * pc:128 * (pc + 1)],
                                     Pt[pc % 2][:, :],
                                     start=(pc == 0), stop=(pc == 31))
                            mm = t.matmul(sump[0:1, :], onecb[:, :],
                                          Pt[pc % 2][:, :],
                                          start=(pc == 0), stop=(pc == 31))
                            if ch == 32:
                                mm.then_inc(tsem, 1)
                        if ch < 32:
                            job, cc = ch // 4, ch % 4
                            sp = sps[ch % 2]
                            t.matmul(sp[:, :], onesr_f[:, :],
                                     crow_ph[0:1, 512 * h:512 * h + 512],
                                     start=True, stop=False)
                            t.matmul(sp[:, :],
                                     ksb[h % 2][:, 128 * ch:128 * (ch + 1)],
                                     qn_ph[:, 512 * h:512 * (h + 1)],
                                     start=False, stop=False)
                            t.matmul(sp[:, :],
                                     kpesb[:, (ph * 8 + job) * 512 + 128 * cc:
                                           (ph * 8 + job) * 512 + 128 * (cc + 1)],
                                     qp_ph[:, 512 * h:512 * (h + 1)],
                                     start=False, stop=False)
                            t.matmul(sp[:, :], isb[:, :],
                                     mskph[:, 512 * ch:512 * (ch + 1)],
                                     start=False, stop=True).then_inc(tsem, 1)

            @blk.scalar
            def _(s):
                for h in range(16):
                    for ch in range(32):
                        s.wait_ge(tsem, t0 + 33 * h + ch + 1)
                        s.activation(Pt[ch % 2][:, :], sps[ch % 2][:, :],
                                     AF.Exp, scale=1.0 / SQD1).then_inc(esem, 1)

            @blk.vector
            def _(v):
                for h in range(16):
                    v.wait_ge(tsem, t0 + 33 * (h + 1))
                    if h >= 1:
                        v.wait_ge(osem, o0 + 32 * h)
                    v.tensor_copy(ob32[:, :], outp[:, :])
                    tt = v.tensor_copy(srow[0:1, :], sump[0:1, :])
                    tt.then_inc(vsem, 1)

            @blk.gpsimd
            def _(g):
                for h in range(16):
                    g.wait_ge(vsem, v0 + h + 1)
                    op = ALU.add if accum else ALU.bypass
                    g.dma_start(out=oacc_dram[128 * h:128 * (h + 1), :],
                                in_=ob32[:, :], accum_op=op).then_inc(osem, 16)
                    g.dma_start(out=sums_dram[h:h + 1, :],
                                in_=srow[0:1, :], accum_op=op).then_inc(osem, 16)
        ctr["t"] += 33 * 16
        ctr["e"] += 32 * 16
        ctr["v"] += 16
        ctr["o"] += 32 * 16
        ctr["k"] += 32 * 16

    def emit_normalize(ph):
        qc0 = 0 if ph == 0 else 512
        nn_ctx = ExitStack()
        of32 = p.sb("nf_%d" % ph, [128, 512], F32, nn_ctx)
        nrow = p.sb("nr_%d" % ph, [1, 512], F32, nn_ctx)
        nbf = p.sb("nb_%d" % ph, [128, 512], BF16, nn_ctx)
        ps_n = p.ps("pn_%d" % ph, [128, 512], ctx=nn_ctx)
        B(nc, sync=lambda sy: sy.wait_ge(osem, ctr["o"]))
        for h in range(16):
            B(nc, sync=lambda sy, h=h: [
                dma(sy, of32[:, :], oacc_dram[128 * h:128 * (h + 1), :]),
                dma(sy, nrow[0:1, :], sums_dram[h:h + 1, :]),
                sy.wait_ge(dsem, p.dcnt)])
            B(nc, vector=lambda v: v.reciprocal(nrow[0:1, :], nrow[0:1, :]))
            B(nc, tensor=lambda t: t.matmul(ps_n[:, :], onesr_f[:, :],
                                            nrow[0:1, :], start=True, stop=True))
            B(nc, vector=lambda v: v.tensor_tensor(nbf[:, :], of32[:, :],
                                                   ps_n[:, :], ALU.mult))
            B(nc, sync=lambda sy, h=h: [
                dma(sy, onorm_dram[128 * h:128 * (h + 1), qc0:qc0 + 512],
                    nbf[:, :]),
                sy.wait_ge(dsem, p.dcnt)])
        nn_ctx.close()

    emit_phase(0)
    emit_normalize(0)
    emit_phase(1)
    emit_phase(2)
    emit_normalize(2)
    fa_ctx.close()

    wo_ctx = ExitStack()
    onsb_all = p.sb("onsb", [128, 16 * TOWN], BF16, wo_ctx)
    wos = [p.sb(f"wos{i}", [128, 16, 128], BF16, wo_ctx) for i in range(2)]
    obuf2 = [p.sb(f"obuf2_{i}", [128, TOWN], F32, wo_ctx) for i in range(2)]
    wop = [p.ps(f"wop{i}", [128, TOWN], ctx=wo_ctx) for i in range(2)]
    onsb = [onsb_all[:, i * TOWN:(i + 1) * TOWN] for i in range(16)]
    with nc.Block() as blk:
        @blk.sync
        def _(sy):
            for i in range(16):
                dma(sy, onsb[i], onorm_dram[128 * i:128 * (i + 1), :])
            sy.wait_ge(dsem, p.dcnt)
    with nc.Block() as blk:
        @blk.gpsimd
        def _(g):
            for o in range(16):
                if o >= 2:
                    g.wait_ge(tsem2, o - 1)
                g.dma_start(out=wos[o % 2][:, :, :],
                            in_=Wout[:, 128 * o:128 * (o + 1)].rearrange(
                                "(hh p) n -> p hh n", p=128)).then_inc(wsem, 16)

        @blk.tensor
        def _(t):
            for o in range(16):
                t.wait_ge(wsem, 16 * (o + 1))
                if o >= 2:
                    t.wait_ge(vsem2, o - 1)
                mm = mm_acc(t, wop[o % 2][:, :],
                            [(wos[o % 2][:, i, :], onsb[i]) for i in range(16)],
                            TOWN)
                mm.then_inc(tsem2, 1)

        @blk.vector
        def _(v):
            for o in range(16):
                v.wait_ge(tsem2, o + 1)
                if o >= 2:
                    v.wait_ge(stsem, 16 * (o - 1))
                v.tensor_copy(obuf2[o % 2][:, :],
                              wop[o % 2][:, :]).then_inc(vsem2, 1)

        @blk.sync
        def _(sy):
            for o in range(16):
                sy.wait_ge(vsem2, o + 1)
                sy.dma_start(out=o_attn[128 * o:128 * (o + 1), :],
                             in_=obuf2[o % 2][:, :]).then_inc(stsem, 16)
            sy.wait_ge(stsem, 16 * 16)
    wo_ctx.close()
    p.ctx.close()
    return nc


def flash1_inputs(full, pre1):
    C = NCORES
    knT = np.concatenate([np.asarray(pre1[c]["knopeT"]) for c in range(C)], axis=1)
    v_full = np.concatenate([np.asarray(pre1[c]["v1"]) for c in range(C)], axis=0)
    kpeT = np.concatenate([np.asarray(pre1[c]["kpeT"]) for c in range(C)], axis=1)
    kn2_full = np.stack([np.asarray(pre1[c]["kn2"]).reshape(16) for c in range(C)],
                        axis=1).reshape(1, 128)
    bf = ml_dtypes.bfloat16
    maps = []
    for c in range(C):
        runsA = list(range(0, c + 1)) + [None] * (7 - c)
        runsB = (list(range(0, 16 - c)) + [None] * 16)[:16]
        jobs = runsA + runsB
        kn_job = np.zeros((16, 3, 128, 4096), dtype=bf)
        v_job = np.zeros((16, 3, 128, 4096), dtype=bf)
        kpe_job = np.zeros((64, 24 * 512), dtype=bf)
        msk = np.zeros((128, 24 * 2048), dtype=np.float32)
        for j, g in enumerate(jobs):
            ph, jp = j // 8, j % 8
            if g is None:
                msk[:, j * 2048:(j + 1) * 2048] = -1e30
                continue
            off = run_owner_offset(g)
            ksl = slice(off, off + 512)
            kt = knT[:, ksl].reshape(16, 128, 512)
            kn_job[:, ph, :, 512 * jp:512 * (jp + 1)] = kt
            vt = v_full[ksl].reshape(4, 128, 16, 128)
            for ks in range(4):
                for h in range(16):
                    v_job[h, ph, :, (jp * 4 + ks) * 128:(jp * 4 + ks + 1) * 128] = \
                        vt[ks, :, h, :]
            kpe_job[:, j * 512:(j + 1) * 512] = kpeT[:, ksl]
            own_g = c if j < 8 else 15 - c
            if g == own_g:
                pp = np.arange(128)[:, None]
                ff = np.arange(512)[None, :]
                for cc in range(4):
                    m = np.where(128 * cc + pp > ff, -1e30, 0.0)
                    msk[:, j * 2048 + 512 * cc: j * 2048 + 512 * (cc + 1)] = m
        maps.append({
            "qn": np.asarray(pre1[c]["qnopeT"]),
            "qp": np.asarray(pre1[c]["qpeT"]),
            "qn2": np.asarray(pre1[c]["qn2"]),
            "kn2": kn2_full,
            "kn_job": kn_job, "v_job": v_job, "kpe_job": kpe_job,
            "msk_job": msk.astype(bf),
            "Wout": f32c(full["L1_attn_w_out"]),
            "I128": CONSTS["I128"], "ones_col": CONSTS["ones_col"],
        })
    return maps


def run_flash1(full, pre1):
    nc = build_flash1()
    res = run_bass_kernel_spmd(nc, flash1_inputs(full, pre1),
                               core_ids=list(range(NCORES)))
    return res.results



def build_moe1():
    p = P()
    nc = p.nc
    xn1T = p.inp("xn1T", [D, TOWN], BF16)
    x1T = p.inp("x1T", [D, TOWN], F32)
    attn1T = p.inp("attn1T", [D, TOWN], F32)
    xlatT = p.inp("xlatT", [1024, TOWN], BF16)
    cwsqT = p.inp("cwsqT", [32, TOWN], F32)
    w1T = p.inp("w1T", [32, 1024, 1024], F32)
    w2T = p.inp("w2T", [32, 1024, 1024], F32)
    fc2T = p.inp("fc2T", [1024, D], F32)
    supT = p.inp("supT", [D, 4096], F32)
    sdnT = p.inp("sdnT", [4096, D], F32)
    c_ones_row = p.inp("ones_row", [1, 128], BF16)

    o_x2 = p.out("x2T", [D, TOWN], F32)

    xn_all = p.sb("xn1", [128, 16 * TOWN], BF16)
    xn = [xn_all[:, i * TOWN:(i + 1) * TOWN] for i in range(16)]
    cwrow = p.sb("cwrow", [1, TOWN], F32)
    onesrf = p.sb("onesrf", [1, 128], F32)
    rl = p.sb("rl", [128, TOWN], F32)
    r2 = p.sb("r2", [128, TOWN], F32)
    cwb = p.sb("cwb", [128, TOWN], F32)
    ob32 = p.sb("ob32", [128, 2 * TOWN], F32)
    ldb = p.sb("ldb", [128, 2 * TOWN], F32)
    f32a = p.sb("f32a", [128, TOWN], F32)
    psA = p.ps("psA", [128, TOWN])
    psB = p.ps("psB", [128, TOWN])
    dsem = p.sem("dsem")
    dtsem = p.sem("dtsem")
    dvsem = p.sem("dvsem")
    osem = p.sem("osem")
    ldsem = p.sem("ldsem")

    def dma(eng, out, in_):
        eng.dma_start(out=out, in_=in_).then_inc(dsem, 16)
        p.dcnt += 16

    def gload(items):
        def gfn(g):
            for out, in_ in items:
                dma(g, out, in_)
            g.wait_ge(dsem, p.dcnt)
        B(nc, gpsimd=gfn)

    for ch in range(16):
        B(nc, sync=lambda sy, ch=ch: [
            dma(sy, f32a[:, :], x1T[128 * ch:128 * (ch + 1), :]),
            dma(sy, ob32[:, TOWN:2 * TOWN], attn1T[128 * ch:128 * (ch + 1), :]),
            sy.wait_ge(dsem, p.dcnt)])
        B(nc, vector=lambda v: v.tensor_tensor(ob32[:, 0:TOWN], f32a[:, :],
                                               ob32[:, TOWN:2 * TOWN], ALU.add))
        B(nc, sync=lambda sy, ch=ch: [
            dma(sy, o_x2[128 * ch:128 * (ch + 1), :], ob32[:, 0:TOWN]),
            sy.wait_ge(dsem, p.dcnt)])

    yl_ctx = ExitStack()
    xlat_all = p.sb("xlat", [128, 8 * TOWN], BF16, yl_ctx)
    xlat = [xlat_all[:, i * TOWN:(i + 1) * TOWN] for i in range(8)]
    ylat_all = p.sb("ylat", [128, 8 * TOWN], F32, yl_ctx)
    ylat = [ylat_all[:, i * TOWN:(i + 1) * TOWN] for i in range(8)]
    ylatb_all = p.sb("ylatb", [128, 8 * TOWN], BF16, yl_ctx)
    ylatb = [ylatb_all[:, i * TOWN:(i + 1) * TOWN] for i in range(8)]
    B(nc, sync=lambda sy: [[dma(sy, xn[i], xn1T[128 * i:128 * (i + 1), :])
                            for i in range(16)],
                           [dma(sy, xlat[i], xlatT[128 * i:128 * (i + 1), :])
                            for i in range(8)],
                           sy.wait_ge(dsem, p.dcnt)],
      vector=lambda v: [v.memset(onesrf[:, :], 1.0),
                        [v.memset(ylat[i][:, :], 0.0) for i in range(8)]])

    ex_ctx = ExitStack()
    w13 = p.sb("w13", [128, 8, 1024], BF16, ex_ctx)
    w23 = p.sb("w23", [128, 8, 1024], BF16, ex_ctx)
    he_all = p.sb("he", [128, 8 * TOWN], BF16, ex_ctx)
    he = [he_all[:, i * TOWN:(i + 1) * TOWN] for i in range(8)]
    for e in range(32):
        gload([(w13[:, i, :], w1T[e, 128 * i:128 * (i + 1), :])
               for i in range(8)])
        B(nc, sync=lambda sy, e=e: [dma(sy, cwrow[0:1, :], cwsqT[e:e + 1, :]),
                                    sy.wait_ge(dsem, p.dcnt)])
        B(nc, tensor=lambda t: mm_acc(
            t, psB[:, :], [(onesrf[:, :], cwrow[0:1, :])], TOWN))
        B(nc, vector=lambda v: v.tensor_copy(cwb[:, :], psB[:, :]))
        for ic in range(8):
            B(nc, tensor=lambda t, ic=ic: mm_acc(
                t, psA[:, :], [(w13[:, i, 128 * ic:128 * (ic + 1)], xlat[i])
                               for i in range(8)], TOWN))
            B(nc, scalar=lambda s: s.activation(rl[:, :], psA[:, :], AF.Relu))
            B(nc, vector=lambda v, ic=ic: [
                v.tensor_tensor(r2[:, :], rl[:, :], cwb[:, :], ALU.mult),
                v.tensor_tensor(he[ic], r2[:, :], rl[:, :], ALU.mult)])
        gload([(w23[:, i, :], w2T[e, 128 * i:128 * (i + 1), :])
               for i in range(8)])
        for lc in range(8):
            B(nc, tensor=lambda t, lc=lc: mm_acc(
                t, psA[:, :], [(w23[:, i, 128 * lc:128 * (lc + 1)], he[i])
                               for i in range(8)], TOWN))
            B(nc, vector=lambda v, lc=lc: v.tensor_tensor(
                ylat[lc], ylat[lc], psA[:, :], ALU.add))
    B(nc, vector=lambda v: [v.tensor_copy(ylatb[i], ylat[i]) for i in range(8)])
    ex_ctx.close()

    fc_ctx = ExitStack()
    wf3 = p.sb("wf3", [128, 8, 2048], BF16, fc_ctx)
    gload([(wf3[:, i, :], fc2T[128 * i:128 * (i + 1), :]) for i in range(8)])
    for oc in range(16):
        B(nc, sync=lambda sy, oc=oc: [
            dma(sy, ldb[:, 0:TOWN], o_x2[128 * oc:128 * (oc + 1), :]),
            sy.wait_ge(dsem, p.dcnt)])
        B(nc, tensor=lambda t, oc=oc: mm_acc(
            t, psA[:, :], [(wf3[:, i, 128 * oc:128 * (oc + 1)], ylatb[i])
                           for i in range(8)], TOWN))
        B(nc, vector=lambda v: v.tensor_tensor(ob32[:, 0:TOWN], ldb[:, 0:TOWN],
                                               psA[:, :], ALU.add))
        B(nc, sync=lambda sy, oc=oc: [
            dma(sy, o_x2[128 * oc:128 * (oc + 1), :], ob32[:, 0:TOWN]),
            sy.wait_ge(dsem, p.dcnt)])

    fc_ctx.close()
    yl_ctx.close()

    wsu = p.sb("wsu", [128, 16, 1024], BF16)
    wsd = p.sb("wsd", [128, 8, 2048], BF16)
    hs_all = p.sb("hs", [128, 8 * TOWN], BF16)
    hs = [hs_all[:, i * TOWN:(i + 1) * TOWN] for i in range(8)]
    for ig in range(4):
        gload([(wsu[:, i, :], supT[128 * i:128 * (i + 1),
                                   1024 * ig:1024 * (ig + 1)])
               for i in range(16)])
        for ic in range(8):
            B(nc, tensor=lambda t, ic=ic: mm_acc(
                t, psA[:, :], [(wsu[:, i, 128 * ic:128 * (ic + 1)], xn[i])
                               for i in range(16)], TOWN))
            B(nc, scalar=lambda s: s.activation(rl[:, :], psA[:, :], AF.Relu))
            B(nc, vector=lambda v, ic=ic: v.tensor_tensor(hs[ic], rl[:, :],
                                                          rl[:, :], ALU.mult))
        gload([(wsd[:, i, :], sdnT[1024 * ig + 128 * i:1024 * ig + 128 * (i + 1), :])
               for i in range(8)])
        for oc in range(16):
            B(nc, sync=lambda sy, oc=oc: [
                dma(sy, ldb[:, 0:TOWN], o_x2[128 * oc:128 * (oc + 1), :]),
                sy.wait_ge(dsem, p.dcnt)])
            B(nc, tensor=lambda t, oc=oc: mm_acc(
                t, psA[:, :], [(wsd[:, i, 128 * oc:128 * (oc + 1)], hs[i])
                               for i in range(8)], TOWN))
            B(nc, vector=lambda v: v.tensor_tensor(ob32[:, 0:TOWN],
                                                   ldb[:, 0:TOWN], psA[:, :],
                                                   ALU.add))
            B(nc, sync=lambda sy, oc=oc: [
                dma(sy, o_x2[128 * oc:128 * (oc + 1), :], ob32[:, 0:TOWN]),
                sy.wait_ge(dsem, p.dcnt)])
    p.ctx.close()
    return nc


def moe1_inputs(full, x1_by_core, pre1, fl1):
    w1T = np.ascontiguousarray(np.asarray(full["L1_ffn_w1"]).transpose(0, 2, 1))
    w2T = np.ascontiguousarray(np.asarray(full["L1_ffn_w2"]).transpose(0, 2, 1))
    fc2T = np.ascontiguousarray(np.asarray(full["L1_ffn_fc2_w"]).T)
    supT = np.ascontiguousarray(np.asarray(full["L1_ffn_shared_up_w"]).T)
    sdnT = np.ascontiguousarray(np.asarray(full["L1_ffn_shared_down_w"]).T)
    maps = []
    for c in range(NCORES):
        maps.append({
            "xn1T": np.asarray(pre1[c]["xn1T"]),
            "x1T": np.asarray(x1_by_core[c]),
            "attn1T": np.asarray(fl1[c]["attn1T"]),
            "xlatT": np.asarray(pre1[c]["xlatT"]),
            "cwsqT": np.asarray(pre1[c]["cwsqT"]),
            "w1T": f32c(w1T), "w2T": f32c(w2T), "fc2T": f32c(fc2T),
            "supT": f32c(supT), "sdnT": f32c(sdnT),
            "ones_row": CONSTS["ones_row"],
        })
    return maps


def run_moe1(full, x1_by_core, pre1, fl1):
    nc = build_moe1()
    res = run_bass_kernel_spmd(nc, moe1_inputs(full, x1_by_core, pre1, fl1),
                               core_ids=list(range(NCORES)))
    return res.results



def kernel(**inputs):
    full = {k: np.asarray(v) for k, v in inputs.items()}
    pre = run_pre0(full)
    pre = [dict(r) for r in pre]
    fl = run_flash0(full, pre)
    deq = run_dequant0(full)
    ffn = run_ffn0(full, pre, fl, deq)
    x1 = [np.asarray(ffn[c]["x1T"]) for c in range(NCORES)]
    pre1 = run_pre1(full, x1)
    pre1 = [dict(r) for r in pre1]
    fl1 = run_flash1(full, pre1)
    moe = run_moe1(full, x1, pre1, fl1)
    out = np.empty((1, S, D), np.float32)
    for c in range(NCORES):
        idx = own_token_index(c)
        out[0, idx, :] = np.asarray(moe[c]["x2T"], np.float32).T
    return out
